# version 28
# baseline (speedup 1.0000x reference)
import numpy as np
from contextlib import ExitStack

import concourse.bass as bass
import concourse.mybir as mybir
from concourse.bass_utils import run_bass_kernel_spmd

F32 = mybir.dt.float32
BF16 = mybir.dt.bfloat16
AF = mybir.ActivationFunctionType
ALU = mybir.AluOpType
AX = mybir.AxisListType

B, L, D, R, HF = 4, 4096, 2048, 256, 8192
TQ = 2048
NSLOT = 16
CJ = [32 - 2 * j for j in range(NSLOT)]
NGRP = 4
GSL = 4
GMAX = [CJ[GSL * g] for g in range(NGRP)]
FT = 1024
NEG = -1.0e30
EPS = 1.0e-5
INV16 = 1.0 / 16.0
FWSCALE = float(1.0 / np.sqrt(HF))

ENGINES = ("tensor", "vector", "scalar", "sync", "gpsimd")


class _Op:
    __slots__ = ("eng", "fn", "deps", "marked", "value", "is_dma", "sem_key",
                 "dma_waits", "snap")

    def __init__(self):
        self.deps = set()
        self.marked = False
        self.value = 0
        self.is_dma = False
        self.sem_key = None
        self.dma_waits = {}


class Emitter:
    def __init__(self):
        self.eng_ops = {e: [] for e in ENGINES}
        self.writer = {}
        self.readers = {}
        self.barrier_ops = []
        self.dma_counts = {}

    def op(self, eng, fn, reads=(), writes=(), dma=None):
        o = _Op()
        o.eng = eng
        o.fn = fn
        if dma is not None:
            o.is_dma = True
            o.sem_key = dma
        deps = o.deps
        for k in reads:
            w = self.writer.get(k)
            if w is not None:
                deps.add(w)
        for k in writes:
            rs = self.readers.get(k)
            if rs:
                deps.update(rs)
            w = self.writer.get(k)
            if w is not None:
                deps.add(w)
        deps.update(self.barrier_ops)
        deps.discard(o)
        if eng == "tensor":
            for p in [p for p in deps if p.eng == "tensor" and not p.is_dma]:
                deps.discard(p)
        for p in list(deps):
            if p.is_dma:
                v = 16 * self.dma_counts[p.sem_key]
                if o.dma_waits.get(p.sem_key, 0) < v:
                    o.dma_waits[p.sem_key] = v
                deps.discard(p)
            else:
                p.marked = True
        if dma is not None:
            self.dma_counts[dma] = self.dma_counts.get(dma, 0) + 1
        for k in reads:
            self.readers.setdefault(k, []).append(o)
        for k in writes:
            self.writer[k] = o
            self.readers[k] = []
        self.eng_ops[eng].append(o)
        return o

    def barrier(self):
        self.barrier_ops = [lst[-1] for lst in self.eng_ops.values() if lst]
        for p in self.barrier_ops:
            if p.is_dma:
                pass
            else:
                p.marked = True

    def finalize(self):
        for e in ("tensor", "vector", "scalar"):
            v = 0
            for o in self.eng_ops[e]:
                if o.marked:
                    v += 1
                    o.value = v

    def replay(self, eng_name, handle, eng_sems, dma_sems):
        waited = {}
        for o in self.eng_ops[eng_name]:
            waits = {}
            for p in o.deps:
                if p.is_dma:
                    continue
                key = ("eng", p.eng)
                if waits.get(key, 0) < p.value:
                    waits[key] = p.value
            for sk, v in o.dma_waits.items():
                key = ("dma", sk)
                if waits.get(key, 0) < v:
                    waits[key] = v
            for key, v in waits.items():
                if waited.get(key, 0) >= v:
                    continue
                waited[key] = v
                sem = eng_sems[key[1]] if key[0] == "eng" else dma_sems[key[1]]
                handle.wait_ge(sem, v)
            ins = o.fn(handle)
            if o.is_dma:
                ins.then_inc(dma_sems[o.sem_key], 16)
            elif o.marked:
                ins.then_inc(eng_sems[o.eng], 1)


def _hadamard(n):
    h = np.array([[1.0]], dtype=np.float32)
    while h.shape[0] < n:
        h = np.block([[h, h], [h, -h]]).astype(np.float32)
    return h


def build_nc():
    nc = bass.Bass("TRN2", target_bir_lowering=False)
    E = Emitter()

    d_xqT = nc.dram_tensor("xqT", [D, TQ], F32, kind="ExternalInput").ap()
    d_xq = nc.dram_tensor("xq", [TQ, D], F32, kind="ExternalInput").ap()
    d_xkT = nc.dram_tensor("xkT", [D, L], F32, kind="ExternalInput").ap()
    d_xk = nc.dram_tensor("xk", [L, D], F32, kind="ExternalInput").ap()
    d_ws = nc.dram_tensor("ws", [D, R], F32, kind="ExternalInput").ap()
    d_wa = nc.dram_tensor("wa", [D, R], F32, kind="ExternalInput").ap()
    d_wot = nc.dram_tensor("wot", [D, D], F32, kind="ExternalInput").ap()
    d_bo = nc.dram_tensor("bo", [1, D], F32, kind="ExternalInput").ap()
    d_w1 = nc.dram_tensor("w1tp", [64, 128, 16 * 128], F32, kind="ExternalInput").ap()
    d_w3 = nc.dram_tensor("w3tp", [64, 128, 16 * 128], F32, kind="ExternalInput").ap()
    d_w2 = nc.dram_tensor("w2ch", [8, 128, 64, 256], F32, kind="ExternalInput").ap()
    d_h128 = nc.dram_tensor("h128", [128, 128], F32, kind="ExternalInput").ap()
    d_i2h64 = nc.dram_tensor("i2h64", [128, 128], F32, kind="ExternalInput").ap()
    d_ident = nc.dram_tensor("ident", [128, 128], F32, kind="ExternalInput").ap()
    d_mask = nc.dram_tensor("mask", [128, 256], F32, kind="ExternalInput").ap()
    d_out = nc.dram_tensor("out", [TQ, D], F32, kind="ExternalOutput").ap()
    d_y1 = nc.dram_tensor("y1d", [TQ, D], F32, kind="Internal").ap()
    d_zz = nc.dram_tensor("zz", [TQ, D], F32, kind="Internal").ap()
    d_vb = nc.dram_tensor("vb", [L, D], BF16, kind="Internal").ap()
    d_hid = nc.dram_tensor("hid", [64, 128, TQ], BF16, kind="Internal").ap()
    d_tmp = nc.dram_tensor("w2tmp", [8, 128, 64, 256], BF16, kind="Internal").ap()
    d_w2p = nc.dram_tensor("w2p", [64, 128, D], BF16, kind="Internal").ap()

    with ExitStack() as S0:
        C = S0.enter_context
        ident_bf = C(nc.sbuf_tensor("ident_bf", [128, 128], BF16))
        ident_f32 = C(nc.sbuf_tensor("ident_f32", [128, 128], F32))
        h128_bf = C(nc.sbuf_tensor("h128_bf", [128, 128], BF16))
        i2h64_bf = C(nc.sbuf_tensor("i2h64_bf", [128, 128], BF16))
        mask_sb = C(nc.sbuf_tensor("mask_sb", [128, 256], F32))
        bo_bf = C(nc.sbuf_tensor("bo_bf", [1, D], BF16))
        ones1 = C(nc.sbuf_tensor("ones1", [1, 128], BF16))
        lns = C(nc.sbuf_tensor("lns", [128, 32], F32))
        nmax = C(nc.sbuf_tensor("nmax", [128, 1], F32))
        sums = C(nc.sbuf_tensor("sums", [128, 1], F32))
        inv1 = C(nc.sbuf_tensor("inv1", [128, 1], F32))

        E.op("gpsimd", lambda g: g.dma_start(out=ident_bf[:], in_=d_ident),
             writes=("ident_bf",), dma="ident")
        E.op("sync", lambda g: g.dma_start(out=ident_f32[:], in_=d_ident),
             writes=("ident_f32",), dma="identf")
        E.op("gpsimd", lambda g: g.dma_start(out=h128_bf[:], in_=d_h128),
             writes=("h128_bf",), dma="h128")
        E.op("gpsimd", lambda g: g.dma_start(out=i2h64_bf[:], in_=d_i2h64),
             writes=("i2h64_bf",), dma="i2h64")
        E.op("gpsimd", lambda g: g.dma_start(out=bo_bf[:], in_=d_bo),
             writes=("bo_bf",), dma="bold")
        E.op("sync", lambda s: s.dma_start(out=mask_sb[:], in_=d_mask),
             writes=("mask_sb",), dma="maskld")
        E.op("vector", lambda v: v.memset(ones1[:], 1.0), writes=("ones1",))

        DC = 256
        NCH = D // DC
        d_tmp_s = d_tmp.rearrange("ch (s l) hi d -> ch s hi l d", s=2)
        d_w2p_s = d_w2p.rearrange("c (s l) t -> s c l t", s=2)
        with ExitStack() as ST:
            CT = ST.enter_context
            rhs1 = [CT(nc.sbuf_tensor(f"rhs1_{i}", [128, 64, DC], BF16)) for i in range(2)]
            t2sb = [CT(nc.sbuf_tensor(f"t2sb_{i}", [128, 64, DC], BF16)) for i in range(1)]
            t1full = [CT(nc.sbuf_tensor(f"t1full_{i}", [128, 64, DC], BF16)) for i in range(1)]
            t2full = [CT(nc.sbuf_tensor(f"t2full_{i}", [128, 64, DC], BF16)) for i in range(1)]
            pt1 = [CT(nc.psum_tensor(f"pt1_{i}", [128, 512], F32)) for i in range(4)]
            pt2 = [CT(nc.psum_tensor(f"pt2_{i}", [128, 512], F32)) for i in range(4)]
            NSC = (64 * DC) // 512

            def emit_load(ch):
                ri = ch % len(rhs1)
                buf = rhs1[ri]
                E.op("gpsimd",
                     (lambda g, ch=ch, buf=buf: g.dma_start(
                         out=buf[:], in_=d_w2[ch])),
                     writes=(f"rhs1:{ri}",), dma=f"w2in{ch % 2}")

            def emit_vb(ch):
                E.op("gpsimd",
                     (lambda g, ch=ch: g.dma_start(
                         out=d_vb[ch * 128:(ch + 1) * 128, :],
                         in_=d_xk[ch * 128:(ch + 1) * 128, :])),
                     writes=(f"vb:{ch}",), dma=f"vbs{ch // 8}")

            def emit_T1(ch):
                ri = ch % len(rhs1)
                buf = rhs1[ri]
                rflat = buf.rearrange("p a b -> p (a b)")
                t1b = t1full[0]
                t1flat = t1b.rearrange("p a b -> p (a b)")
                for c in range(NSC):
                    ps = pt1[c % 4]
                    E.op("tensor",
                         (lambda t, c=c, ps=ps, rflat=rflat:
                          t.matmul(ps[:], h128_bf[:],
                                   rflat[:, c * 512:(c + 1) * 512],
                                   start=True, stop=True,
                                   skip_group_check=True)),
                         reads=(f"rhs1:{ri}", "h128_bf"),
                         writes=(f"pt1{c % 4}",))
                    if c % 2 == 0:
                        E.op("vector",
                             (lambda h, c=c, ps=ps, t1flat=t1flat: h.tensor_scalar(
                                 t1flat[:, c * 512:(c + 1) * 512],
                                 ps[:], FWSCALE, None, ALU.mult)),
                             reads=(f"pt1{c % 4}",), writes=(f"t1f:0:{c}",))
                    else:
                        E.op("scalar",
                             (lambda h, c=c, ps=ps, t1flat=t1flat: h.activation(
                                 t1flat[:, c * 512:(c + 1) * 512],
                                 ps[:], AF.Copy, scale=FWSCALE)),
                             reads=(f"pt1{c % 4}",), writes=(f"t1f:0:{c}",))
                E.op("sync",
                     (lambda s, ch=ch, t1b=t1b: s.dma_start(
                         out=d_tmp[ch], in_=t1b[:])),
                     reads=tuple(f"t1f:0:{c}" for c in range(NSC)),
                     writes=(f"tmpD:{ch}",), dma=f"tmpw{ch % 2}")

            def emit_T2(ch):
                d0 = ch * DC
                buf = t2sb[0]
                rflat = buf.rearrange("p a b -> p (a b)")
                t2b = t2full[0]
                t2flat = t2b.rearrange("p a b -> p (a b)")
                for sh in range(2):
                    E.op("sync",
                         (lambda s, ch=ch, buf=buf, sh=sh: s.dma_start(
                             out=buf[sh * 64:(sh + 1) * 64, :, :],
                             in_=d_tmp_s[ch, sh])),
                         reads=(f"tmpD:{ch}",),
                         writes=("t2sb:0",), dma=f"tmpr{ch % 2}")
                for c in range(NSC):
                    ps = pt2[c % 4]
                    E.op("tensor",
                         (lambda t, c=c, ps=ps, rflat=rflat:
                          t.matmul(ps[:], i2h64_bf[:],
                                   rflat[:, c * 512:(c + 1) * 512],
                                   start=True, stop=True,
                                   skip_group_check=True)),
                         reads=("t2sb:0", "i2h64_bf"),
                         writes=(f"pt2{c % 4}",))
                    if c % 2 == 0:
                        E.op("vector",
                             (lambda h, c=c, ps=ps, t2flat=t2flat: h.tensor_copy(
                                 t2flat[:, c * 512:(c + 1) * 512], ps[:])),
                             reads=(f"pt2{c % 4}",), writes=(f"t2f:0:{c}",))
                    else:
                        E.op("scalar",
                             (lambda h, c=c, ps=ps, t2flat=t2flat: h.activation(
                                 t2flat[:, c * 512:(c + 1) * 512], ps[:], AF.Copy)),
                             reads=(f"pt2{c % 4}",), writes=(f"t2f:0:{c}",))
                for sh in range(2):
                    E.op("sync",
                         (lambda s, ch=ch, d0=d0, sh=sh, t2b=t2b: s.dma_start(
                             out=d_w2p_s[sh, :, :, d0:d0 + DC],
                             in_=t2b[sh * 64:(sh + 1) * 64, :, :])),
                         reads=tuple(f"t2f:0:{c}" for c in range(NSC)),
                         writes=(f"w2p:{ch}",), dma="w2pw")

            emit_load(0)
            emit_load(1)
            for ch in range(NCH):
                emit_T1(ch)
                if ch + 2 < NCH:
                    emit_load(ch + 2)
                for q in range(4):
                    emit_vb(4 * ch + q)
                if ch >= 2:
                    emit_T2(ch - 2)
            emit_T2(NCH - 2)
            emit_T2(NCH - 1)
            E.barrier()

        SA = ExitStack()
        CA = SA.enter_context
        st_bf = [CA(nc.sbuf_tensor(f"st{r}", [128, TQ], BF16)) for r in range(2)]
        ac_bf = [CA(nc.sbuf_tensor(f"ac{r}", [128, L], BF16)) for r in range(2)]
        with ExitStack() as S1:
            C1 = S1.enter_context
            ws_bf = C1(nc.sbuf_tensor("ws_bf", [128, 16 * R], BF16))
            wa_bf = C1(nc.sbuf_tensor("wa_bf", [128, 16 * R], BF16))
            xsb = [C1(nc.sbuf_tensor(f"xsb{i}", [128, 2048], BF16)) for i in range(2)]
            pj = [C1(nc.psum_tensor(f"pj{i}", [128, 512], F32)) for i in range(8)]

            for k in range(16):
                E.op("gpsimd",
                     (lambda g, k=k: g.dma_start(
                         out=ws_bf[:, k * R:(k + 1) * R],
                         in_=d_ws[k * 128:(k + 1) * 128, :])),
                     writes=(f"ws:{k}",), dma="wsld")
                E.op("gpsimd",
                     (lambda g, k=k: g.dma_start(
                         out=wa_bf[:, k * R:(k + 1) * R],
                         in_=d_wa[k * 128:(k + 1) * 128, :])),
                     writes=(f"wa:{k}",), dma="wald")

            def proj(which, wbuf, wkey, src, src_col0, dst, dst_col0):
                for k in range(16):
                    buf = xsb[k % 2]
                    E.op("gpsimd",
                         (lambda g, k=k, buf=buf: g.dma_start(
                             out=buf[:],
                             in_=src[k * 128:(k + 1) * 128,
                                     src_col0:src_col0 + 2048])),
                         reads=(), writes=(f"xsb{k % 2}",), dma=f"xsb{k % 2}")
                    for m in range(2):
                        for n in range(4):
                            p = pj[m * 4 + n]
                            E.op("tensor",
                                 (lambda t, k=k, m=m, n=n, p=p, buf=buf:
                                  t.matmul(p[:],
                                           wbuf[:, k * R + m * 128:k * R + (m + 1) * 128],
                                           buf[:, n * 512:(n + 1) * 512],
                                           start=(k == 0), stop=(k == 15),
                                           skip_group_check=True)),
                                 reads=(f"{wkey}:{k}", f"xsb{k % 2}"),
                                 writes=(f"pj{m * 4 + n}",))
                for m in range(2):
                    for n in range(4):
                        p = pj[m * 4 + n]
                        scale = INV16 if which == "state" else None
                        if scale is not None:
                            E.op("vector",
                                 (lambda v, m=m, n=n, p=p:
                                  v.tensor_scalar(
                                      dst[m][:, dst_col0 + n * 512:dst_col0 + (n + 1) * 512],
                                      p[:], INV16, None, ALU.mult)),
                                 reads=(f"pj{m * 4 + n}",),
                                 writes=(f"{'st' if which == 'state' else 'ac'}:{m}:{dst_col0 + n * 512}",))
                        else:
                            E.op("vector",
                                 (lambda v, m=m, n=n, p=p:
                                  v.tensor_copy(
                                      dst[m][:, dst_col0 + n * 512:dst_col0 + (n + 1) * 512],
                                      p[:])),
                                 reads=(f"pj{m * 4 + n}",),
                                 writes=(f"ac:{m}:{dst_col0 + n * 512}",))

            proj("state", ws_bf, "ws", d_xqT, 0, st_bf, 0)
            proj("action", wa_bf, "wa", d_xkT, 0, ac_bf, 0)
            proj("action", wa_bf, "wa", d_xkT, 2048, ac_bf, 2048)
            E.barrier()

        with ExitStack() as S2:
            C2 = S2.enter_context
            wot_bf = C2(nc.sbuf_tensor("wot_bf", [128, 16 * D], BF16))
            Praw = C2(nc.sbuf_tensor("Praw", [128, 4096], F32))
            Pbf2 = [C2(nc.sbuf_tensor(f"Pbf{i}", [128, 4096], BF16)) for i in range(2)]
            PT2 = [C2(nc.sbuf_tensor(f"PT{i}", [128, 32 * 256], BF16)) for i in range(2)]
            atT = C2(nc.sbuf_tensor("atT", [128, 16 * 256], BF16))
            v_c = [C2(nc.sbuf_tensor(f"vc{i}", [128, 1024], BF16)) for i in range(2)]
            z1b = [C2(nc.sbuf_tensor(f"z1b{i}", [128, D], F32)) for i in range(2)]
            y1b = C2(nc.sbuf_tensor("y1b", [128, D], F32))
            xqt = C2(nc.sbuf_tensor("xqt", [128, D], F32))
            ptps = [C2(nc.psum_tensor(f"ptps{i}", [128, 128], BF16)) for i in range(2)]
            sc = [C2(nc.psum_tensor(f"sc{i}", [128, 512], F32)) for i in range(2)]
            av = [C2(nc.psum_tensor(f"av{i}", [128, 512], F32)) for i in range(4)]

            for k in range(16):
                E.op("gpsimd",
                     (lambda g, k=k: g.dma_start(
                         out=wot_bf[:, k * D:(k + 1) * D],
                         in_=d_wot[k * 128:(k + 1) * 128, :])),
                     writes=(f"wot:{k}",), dma="wotld")

            def emit_scores(p, i):
                j = 2 * p + i
                cj = CJ[j]
                W = cj * 128
                Pbf = Pbf2[i]
                pbk = f"pbf{i}"
                off = 0
                bi = 0
                while off < W:
                    wdt = min(512, W - off)
                    ps = sc[bi % 2]
                    for r in range(2):
                        E.op("tensor",
                             (lambda t, r=r, j=j, off=off, wdt=wdt, ps=ps:
                              t.matmul(ps[:, 0:wdt],
                                       st_bf[r][:, j * 128:(j + 1) * 128],
                                       ac_bf[r][:, off:off + wdt],
                                       start=(r == 0), stop=(r == 1))),
                             reads=(f"st:{r}:{(j * 128) // 512 * 512}",
                                    f"ac:{r}:{(off // 512) * 512}",
                                    f"ac:{r}:{((off + wdt - 1) // 512) * 512}"),
                             writes=(f"scp{bi % 2}",))
                    if off + wdt == W:
                        if wdt == 512:
                            E.op("vector",
                                 (lambda v, off=off, ps=ps:
                                  v.tensor_copy(Praw[:, off:off + 256], ps[:, 0:256])),
                                 reads=(f"scp{bi % 2}",), writes=("praw",))
                            E.op("vector",
                                 (lambda v, off=off, ps=ps:
                                  v.tensor_tensor(Praw[:, off + 256:off + 512],
                                                  ps[:, 256:512], mask_sb[:], ALU.add)),
                                 reads=(f"scp{bi % 2}", "mask_sb"), writes=("praw",))
                        else:
                            E.op("vector",
                                 (lambda v, off=off, ps=ps:
                                  v.tensor_tensor(Praw[:, off:off + 256],
                                                  ps[:, 0:256], mask_sb[:], ALU.add)),
                                 reads=(f"scp{bi % 2}", "mask_sb"), writes=("praw",))
                    else:
                        E.op("vector",
                             (lambda v, off=off, wdt=wdt, ps=ps:
                              v.tensor_copy(Praw[:, off:off + wdt], ps[:, 0:wdt])),
                             reads=(f"scp{bi % 2}",), writes=("praw",))
                    off += wdt
                    bi += 1
                E.op("vector",
                     (lambda v, W=W: v.reduce_max(nmax[:], Praw[:, 0:W],
                                                  axis=AX.X, negate=True)),
                     reads=("praw",), writes=("nmax",))
                E.op("scalar",
                     (lambda s, W=W, Pbf=Pbf: s.activation(
                         Pbf[:, 0:W], Praw[:, 0:W], AF.Exp, bias=nmax[:],
                         accum_out=sums[:])),
                     reads=("praw", "nmax"), writes=(pbk, "sums"))
                E.op("vector", lambda v: v.reciprocal(inv1[:], sums[:]),
                     reads=("sums",), writes=("inv1",))
                E.op("vector",
                     (lambda v, W=W, Pbf=Pbf: v.tensor_scalar_mul(
                         Pbf[:, 0:W], Pbf[:, 0:W], inv1[:])),
                     reads=(pbk, "inv1"), writes=(pbk,))

            def emit_av_memsets(p, half):
                for b in range(4):
                    E.op("vector",
                         (lambda v, b=b: v.memset(av[b][:], 0.0)),
                         writes=(f"av{b}",))

            def emit_av_half(p, half, inter=None):
                maxp = CJ[2 * p]
                for ch in range(maxp):
                    buf = v_c[ch % 2]
                    E.op("sync",
                         (lambda g2, ch=ch, half=half, buf=buf: g2.dma_start(
                             out=buf[:],
                             in_=d_vb[ch * 128:(ch + 1) * 128,
                                      half * 1024:(half + 1) * 1024])),
                         reads=(f"vb:{ch}",),
                         writes=(f"vc{ch % 2}",), dma=f"vc{ch % 2}")
                    for m in range(8):
                        E.op("tensor",
                             (lambda t, ch=ch, m=m, maxp=maxp, buf=buf, p=p:
                              t.matmul(av[m // 2][:, (m % 2) * 256:(m % 2) * 256 + 256],
                                       buf[:, m * 128:(m + 1) * 128],
                                       PT2[p % 2][:, ch * 256:(ch + 1) * 256],
                                       start=False, stop=(ch == maxp - 1),
                                       skip_group_check=True)),
                             reads=(f"vc{ch % 2}", f"pt:{ch}:0", f"pt:{ch}:1"),
                             writes=(f"av{m // 2}",))
                    if inter:
                        n_per = max(1, len(inter) // maxp + 1)
                        for _ in range(n_per):
                            if inter:
                                inter.pop(0)()
                for m in range(8):
                    hc = half * 8 + m
                    E.op("vector",
                         (lambda v, hc=hc, m=m:
                          v.tensor_copy(atT[:, hc * 256:(hc + 1) * 256],
                                        av[m // 2][:, (m % 2) * 256:(m % 2) * 256 + 256])),
                         reads=(f"av{m // 2}",), writes=(f"atT:{hc}",))
                if inter:
                    while inter:
                        inter.pop(0)()

            def pt_thunks(p):
                maxp = CJ[2 * p]
                thunks = []
                for i in range(2):
                    cj = CJ[2 * p + i]
                    Pbf = Pbf2[i]
                    pbk = f"pbf{i}"
                    for kb in range(cj):
                        pp = ptps[(len(thunks)) % 2]
                        def th(kb=kb, i=i, pp=pp, Pbf=Pbf, pbk=pbk, p=p,
                               ppk=f"ptps{len(thunks) % 2}"):
                            E.op("tensor",
                                 (lambda t: t.transpose(
                                     pp[:], Pbf[:, kb * 128:(kb + 1) * 128],
                                     ident_bf[:])),
                                 reads=(pbk, "ident_bf"), writes=(ppk,))
                            E.op("vector",
                                 (lambda v: v.tensor_copy(
                                     PT2[p % 2][:, kb * 256 + i * 128:kb * 256 + (i + 1) * 128],
                                     pp[:])),
                                 reads=(ppk,), writes=(f"pt:{kb}:{i}",))
                        thunks.append(th)
                    for kb in range(cj, maxp):
                        def thz(kb=kb, i=i, p=p):
                            E.op("vector",
                                 (lambda v: v.memset(
                                     PT2[p % 2][:, kb * 256 + i * 128:kb * 256 + (i + 1) * 128], 0.0)),
                                 writes=(f"pt:{kb}:{i}",))
                        thunks.append(thz)
                return thunks

            def emit_wo_epi(p):
                for i in range(2):
                    j = 2 * p + i
                    E.op("sync",
                         (lambda s, j=j: s.dma_start(
                             out=xqt[:], in_=d_xq[j * 128:(j + 1) * 128, :])),
                         writes=("xqt",), dma="xqtld")
                    z1 = z1b[i]
                    zkey = f"z1b{i}"
                    for n in range(4):
                        ps = sc[n % 2]
                        for k in range(16):
                            E.op("tensor",
                                 (lambda t, k=k, n=n, i=i, ps=ps:
                                  t.matmul(ps[:],
                                           atT[:, k * 256 + i * 128:k * 256 + (i + 1) * 128],
                                           wot_bf[:, k * D + n * 512:k * D + (n + 1) * 512],
                                           start=(k == 0), stop=False,
                                           skip_group_check=True)),
                                 reads=(f"atT:{k}", f"wot:{k}"),
                                 writes=(f"scp{n % 2}",))
                        E.op("tensor",
                             (lambda t, n=n, ps=ps:
                              t.matmul(ps[:], ones1[:],
                                       bo_bf[:, n * 512:(n + 1) * 512],
                                       start=False, stop=True,
                                       skip_group_check=True)),
                             reads=("ones1", "bo_bf"), writes=(f"scp{n % 2}",))
                        E.op("vector",
                             (lambda v, n=n, ps=ps, z1=z1:
                              v.tensor_tensor(z1[:, n * 512:(n + 1) * 512], ps[:],
                                              xqt[:, n * 512:(n + 1) * 512], ALU.add)),
                             reads=(f"scp{n % 2}", "xqt"), writes=(zkey,))
                _ln_pair(E, z1b, ("z1b0", "z1b1"), y1b, lns, Praw, d_y1, p)

            order = [7, 6, 5, 4, 3, 2, 1, 0]
            for step in range(9):
                prev = order[step - 1] if step >= 1 else None
                cur = order[step] if step < 8 else None
                if prev is not None:
                    emit_av_memsets(prev, 0)
                if cur is not None:
                    emit_scores(cur, 0)
                    emit_scores(cur, 1)
                if prev is not None:
                    emit_av_half(prev, 0)
                    emit_av_memsets(prev, 1)
                    emit_av_half(prev, 1,
                                 inter=(pt_thunks(cur) if cur is not None else None))
                    emit_wo_epi(prev)
                elif cur is not None:
                    for th in pt_thunks(cur):
                        th()
            E.barrier()
        SA.close()

        with ExitStack() as SB1:
            CB = SB1.enter_context
            y1t = CB(nc.sbuf_tensor("y1t", [128, 16 * TQ], BF16))
            w1b = [CB(nc.sbuf_tensor(f"w1b{i}", [128, 16 * 128], BF16)) for i in range(2)]
            w3b = [CB(nc.sbuf_tensor(f"w3b{i}", [128, 16 * 128], BF16)) for i in range(2)]
            y1bf = [CB(nc.sbuf_tensor(f"y1bf{i}", [128, D], BF16)) for i in range(2)]
            hstg = [CB(nc.sbuf_tensor(f"hstg{i}", [128, 512], BF16)) for i in range(4)]
            silu = [CB(nc.sbuf_tensor(f"silu{i}", [128, 512], BF16)) for i in range(2)]
            trps = CB(nc.psum_tensor("trps", [128, 128], BF16))
            up = [CB(nc.psum_tensor(f"up{i}", [128, 512], F32)) for i in range(4)]

            for m in range(16):
                buf = y1bf[m % 2]
                E.op("gpsimd",
                     (lambda g2, m=m, buf=buf: g2.dma_start(
                         out=buf[:],
                         in_=d_y1[m * 128:(m + 1) * 128, :])),
                     reads=(f"y1d:{m}",),
                     writes=(f"y1bf{m % 2}",), dma=f"y1bf{m % 2}")
                for c in range(16):
                    E.op("tensor",
                         (lambda t, c=c, buf=buf, m=m:
                          t.transpose(trps[:], buf[:, c * 128:(c + 1) * 128],
                                      ident_bf[:])),
                         reads=(f"y1bf{m % 2}", "ident_bf"), writes=("trps",))
                    E.op("vector",
                         (lambda v, c=c, m=m:
                          v.tensor_copy(
                              y1t[:, c * TQ + m * 128:c * TQ + (m + 1) * 128],
                              trps[:])),
                         reads=("trps",), writes=(f"y1t:{c}:{m}",))
            for hc in range(64):
                bw1, bw3 = w1b[hc % 2], w3b[hc % 2]
                E.op("gpsimd",
                     (lambda g2, hc=hc, bw1=bw1: g2.dma_start(
                         out=bw1[:], in_=d_w1[hc])),
                     writes=(f"w1b{hc % 2}",), dma=f"w1b{hc % 2}")
                E.op("gpsimd",
                     (lambda g2, hc=hc, bw3=bw3: g2.dma_start(
                         out=bw3[:], in_=d_w3[hc])),
                     writes=(f"w3b{hc % 2}",), dma=f"w3b{hc % 2}")
                for n in range(4):
                    p1 = up[n % 2]
                    p3 = up[2 + n % 2]
                    for k in range(16):
                        E.op("tensor",
                             (lambda t, k=k, n=n, p1=p1, bw1=bw1:
                              t.matmul(p1[:],
                                       bw1[:, k * 128:(k + 1) * 128],
                                       y1t[:, k * TQ + n * 512:k * TQ + n * 512 + 512],
                                       start=(k == 0), stop=(k == 15),
                                       skip_group_check=True)),
                             reads=(f"w1b{hc % 2}",) + tuple(
                                 f"y1t:{k}:{n * 4 + mm}" for mm in range(4)),
                             writes=(f"up{n % 2}",))
                    for k in range(16):
                        E.op("tensor",
                             (lambda t, k=k, n=n, p3=p3, bw3=bw3:
                              t.matmul(p3[:],
                                       bw3[:, k * 128:(k + 1) * 128],
                                       y1t[:, k * TQ + n * 512:k * TQ + n * 512 + 512],
                                       start=(k == 0), stop=(k == 15),
                                       skip_group_check=True)),
                             reads=(f"w3b{hc % 2}",) + tuple(
                                 f"y1t:{k}:{n * 4 + mm}" for mm in range(4)),
                             writes=(f"up{2 + n % 2}",))
                    sl = silu[n % 2]
                    hb = hstg[n]
                    E.op("scalar",
                         (lambda s, p1=p1, sl=sl: s.activation(sl[:], p1[:], AF.Silu)),
                         reads=(f"up{n % 2}",), writes=(f"silu{n % 2}",))
                    E.op("vector",
                         (lambda v, p3=p3, sl=sl, hb=hb:
                          v.tensor_tensor(hb[:], sl[:], p3[:], ALU.mult)),
                         reads=(f"silu{n % 2}", f"up{2 + n % 2}"),
                         writes=(f"hstg{n}",))
                    E.op("sync",
                         (lambda s2, hc=hc, n=n, hb=hb: s2.dma_start(
                             out=d_hid[hc][:, n * 512:(n + 1) * 512], in_=hb[:])),
                         reads=(f"hstg{n}",), writes=(f"hid:{hc}:{n}",), dma="hidw")
            E.barrier()

        FT2 = 512
        with ExitStack() as SB2:
            CB2 = SB2.enter_context
            Hd2r = [CB2(nc.sbuf_tensor(f"Hd2_{i}", [128, 64 * FT2], BF16)) for i in range(2)]
            zt = CB2(nc.sbuf_tensor("zt", [128, D], F32))
            yt = CB2(nc.sbuf_tensor("yt", [128, D], F32))
            ot = CB2(nc.sbuf_tensor("ot", [128, D], F32))
            sqc = CB2(nc.sbuf_tensor("sqc", [128, D], BF16))
            w2h = [CB2(nc.sbuf_tensor(f"w2h{i}", [128, 8 * 512], BF16)) for i in range(3)]
            z2s = [CB2(nc.sbuf_tensor(f"z2s{i}", [128, 512], F32)) for i in range(2)]
            dps = [CB2(nc.psum_tensor(f"dps{i}", [128, 512], F32)) for i in range(8)]

            def emit_C(tc):
                for mc in range(tc * 4, tc * 4 + 4):
                    E.op("sync",
                         (lambda s, mc=mc: s.dma_start(
                             out=zt[:], in_=d_zz[mc * 128:(mc + 1) * 128, :])),
                         reads=tuple(f"zz:{mc // 4}:{mc % 4}:{n}" for n in range(4)),
                         writes=("zt",), dma="ztld")
                    E.op("gpsimd",
                         (lambda s, mc=mc: s.dma_start(
                             out=yt[:], in_=d_y1[mc * 128:(mc + 1) * 128, :])),
                         reads=(f"y1d:{mc}",), writes=("yt",), dma="ytld")
                    E.op("vector",
                         (lambda v: v.tensor_add(zt[:], zt[:], yt[:])),
                         reads=("zt", "yt"), writes=("zt",))
                    _ln_norm(E, nc, zt, "zt", ot, "ot", lns, sqc, "sqc")
                    E.op("sync",
                         (lambda s, mc=mc: s.dma_start(
                             out=d_out[mc * 128:(mc + 1) * 128, :], in_=ot[:])),
                         reads=("ot",), writes=(f"out:{mc}",), dma="outw")

            d_hid_r = d_hid.rearrange("c p t -> p c t")
            d_w2p_r = d_w2p.rearrange("c p t -> p c t")
            def emit_h2l(tt):
                Hd2 = Hd2r[tt % 2]
                for hc0 in range(0, 64, 8):
                    rd = tuple(f"hid:{hc}:{tt}" for hc in range(hc0, hc0 + 8))
                    E.op("gpsimd",
                         (lambda s2, hc0=hc0, tt=tt, Hd2=Hd2: s2.dma_start(
                             out=Hd2[:, hc0 * FT2:(hc0 + 8) * FT2],
                             in_=d_hid_r[:, hc0:hc0 + 8,
                                         tt * FT2:(tt + 1) * FT2])),
                         reads=rd, writes=(f"Hd2:{tt % 2}:{hc0}",), dma="h2l")

            emit_h2l(0)
            emit_h2l(1)
            for p in range(2):
                if p == 1:
                    emit_h2l(2)
                    emit_h2l(3)
                    emit_C(0)
                    emit_C(1)
                for n in range(4):
                    for qb in range(8):
                        s_idx = n * 8 + qb
                        buf = w2h[s_idx % 3]
                        hh0 = qb * 8
                        E.op("gpsimd",
                             (lambda g2, hh0=hh0, n=n, buf=buf: g2.dma_start(
                                 out=buf[:],
                                 in_=d_w2p_r[:, hh0:hh0 + 8,
                                             n * 512:(n + 1) * 512])),
                             reads=tuple(f"w2p:{ch}"
                                         for ch in range(2 * n, 2 * n + 2)),
                             writes=(f"w2q{s_idx % 3}",),
                             dma=f"w2h{s_idx % 3}")
                        for sub in range(2):
                            Hd2 = Hd2r[sub]
                            dbase = 4 * sub
                            for m in range(4):
                                for h2 in range(8):
                                    hh = hh0 + h2
                                    E.op("tensor",
                                         (lambda t, m=m, h2=h2, hh=hh, qb=qb,
                                          buf=buf, Hd2=Hd2, dbase=dbase:
                                          t.matmul(dps[dbase + m][:],
                                                   Hd2[:, hh * FT2 + m * 128:hh * FT2 + (m + 1) * 128],
                                                   buf[:, h2 * 512:(h2 + 1) * 512],
                                                   start=(qb == 0 and h2 == 0),
                                                   stop=(qb == 7 and h2 == 7),
                                                   skip_group_check=True)),
                                         reads=(f"w2q{s_idx % 3}", f"Hd2:{sub}:{hh0}"),
                                         writes=(f"dps{dbase + m}",))
                    for sub in range(2):
                        tt = 2 * p + sub
                        dbase = 4 * sub
                        for m in range(4):
                            zb = z2s[m % 2]
                            E.op("vector",
                                 (lambda v, m=m, zb=zb, dbase=dbase:
                                  v.tensor_copy(zb[:], dps[dbase + m][:])),
                                 reads=(f"dps{dbase + m}",), writes=(f"z2s{m % 2}",))
                            E.op("sync",
                                 (lambda s, tt=tt, m=m, n=n, zb=zb: s.dma_start(
                                     out=d_zz[tt * FT2 + m * 128:tt * FT2 + (m + 1) * 128,
                                              n * 512:(n + 1) * 512],
                                     in_=zb[:])),
                                 reads=(f"z2s{m % 2}",), writes=(f"zz:{tt}:{m}:{n}",),
                                 dma="zzw")
            emit_C(2)
            emit_C(3)

        E.finalize()

        eng_sems = {}
        dma_sems = {}
        for e in ("tensor", "vector", "scalar"):
            eng_sems[e] = C(nc.semaphore(f"sem_{e}"))
        for sk in E.dma_counts:
            dma_sems[sk] = C(nc.semaphore(f"ds_{sk}"))

        block = C(nc.Block())
        n_out = E.dma_counts["outw"]

        @block.tensor
        def _(t):
            E.replay("tensor", t, eng_sems, dma_sems)

        @block.vector
        def _(v):
            E.replay("vector", v, eng_sems, dma_sems)

        @block.scalar
        def _(s):
            E.replay("scalar", s, eng_sems, dma_sems)

        @block.gpsimd
        def _(g):
            E.replay("gpsimd", g, eng_sems, dma_sems)

        @block.sync
        def _(s):
            E.replay("sync", s, eng_sems, dma_sems)
            s.wait_ge(dma_sems["outw"], 16 * n_out)

    return nc


def _ln_pair(E, z1b, zkeys, y1b, lns, praw, d_y1, p):
    S1, SSQ, MEAN, MSQ, M2, VAR, VPE, SD, R0, RR, H, H2, RSTD, MR, NMR = range(15)

    def col(c, i):
        return lns[:, c + 16 * i:c + 16 * i + 1]

    def vop(i, fn, rd, wr):
        E.op("vector", fn, reads=rd, writes=wr)

    for i in range(2):
        z, zk = z1b[i], zkeys[i]
        E.op("vector", (lambda v, z=z, i=i: v.reduce_sum(col(S1, i), z[:], axis=AX.X)),
             reads=(zk,), writes=(f"ln{i}:s1",))
    for i in range(2):
        z, zk = z1b[i], zkeys[i]
        E.op("scalar",
             (lambda s, z=z, i=i: s.activation(praw[:, 2048:2048 + D], z[:], AF.Square,
                                               accum_out=col(SSQ, i))),
             reads=(zk,), writes=("praw", f"ln{i}:ssq"))
    for i in range(2):
        E.op("vector", (lambda v, i=i: v.tensor_scalar_mul(col(MEAN, i), col(S1, i), 1.0 / D)),
             reads=(f"ln{i}:s1",), writes=(f"ln{i}:mean",))
        E.op("vector", (lambda v, i=i: v.tensor_scalar_mul(col(MSQ, i), col(SSQ, i), 1.0 / D)),
             reads=(f"ln{i}:ssq",), writes=(f"ln{i}:msq",))
        E.op("vector", (lambda v, i=i: v.tensor_mul(col(M2, i), col(MEAN, i), col(MEAN, i))),
             reads=(f"ln{i}:mean",), writes=(f"ln{i}:m2",))
        E.op("vector", (lambda v, i=i: v.tensor_sub(col(VAR, i), col(MSQ, i), col(M2, i))),
             reads=(f"ln{i}:msq", f"ln{i}:m2"), writes=(f"ln{i}:var",))
        E.op("vector", (lambda v, i=i: v.tensor_scalar_add(col(VPE, i), col(VAR, i), EPS)),
             reads=(f"ln{i}:var",), writes=(f"ln{i}:vpe",))
    for i in range(2):
        E.op("scalar", (lambda s, i=i: s.activation(col(SD, i), col(VPE, i), AF.Sqrt)),
             reads=(f"ln{i}:vpe",), writes=(f"ln{i}:sd",))
    for i in range(2):
        E.op("vector", (lambda v, i=i: v.reciprocal(col(R0, i), col(SD, i))),
             reads=(f"ln{i}:sd",), writes=(f"ln{i}:r0",))
        E.op("vector", (lambda v, i=i: v.tensor_mul(col(RR, i), col(R0, i), col(R0, i))),
             reads=(f"ln{i}:r0",), writes=(f"ln{i}:rr",))
        E.op("vector", (lambda v, i=i: v.tensor_mul(col(H, i), col(RR, i), col(VPE, i))),
             reads=(f"ln{i}:rr", f"ln{i}:vpe"), writes=(f"ln{i}:h",))
        E.op("vector", (lambda v, i=i: v.tensor_scalar(col(H2, i), col(H, i), -0.5, 1.5,
                                                       ALU.mult, ALU.add)),
             reads=(f"ln{i}:h",), writes=(f"ln{i}:h2",))
        E.op("vector", (lambda v, i=i: v.tensor_mul(col(RSTD, i), col(R0, i), col(H2, i))),
             reads=(f"ln{i}:r0", f"ln{i}:h2"), writes=(f"ln{i}:rstd",))
        E.op("vector", (lambda v, i=i: v.tensor_mul(col(MR, i), col(MEAN, i), col(RSTD, i))),
             reads=(f"ln{i}:mean", f"ln{i}:rstd"), writes=(f"ln{i}:mr",))
        E.op("vector", (lambda v, i=i: v.tensor_scalar_mul(col(NMR, i), col(MR, i), -1.0)),
             reads=(f"ln{i}:mr",), writes=(f"ln{i}:nmr",))
    for i in range(2):
        j = 2 * p + i
        z, zk = z1b[i], zkeys[i]
        E.op("scalar",
             (lambda s, z=z, i=i: s.activation(y1b[:], z[:], AF.Identity,
                                               bias=col(NMR, i), scale=col(RSTD, i))),
             reads=(zk, f"ln{i}:nmr", f"ln{i}:rstd"), writes=("y1b",))
        E.op("sync",
             (lambda s, j=j: s.dma_start(
                 out=d_y1[j * 128:(j + 1) * 128, :], in_=y1b[:])),
             reads=("y1b",), writes=(f"y1d:{j}",), dma="y1w")


def _ln_norm(E, nc, z, zkey, yout, ykey, lns, sq_scratch, sqkey):
    S1, SSQ, MEAN, MSQ, M2, VAR, VPE, SD, R0, RR, H, H2, RSTD, MR, NMR = range(15)

    def col(c):
        return lns[:, c:c + 1]

    E.op("vector", lambda v: v.reduce_sum(col(S1), z[:], axis=AX.X),
         reads=(zkey,), writes=("ln:s1",))
    E.op("scalar",
         (lambda s: s.activation(sq_scratch[:, 0:D], z[:], AF.Square,
                                 accum_out=col(SSQ))),
         reads=(zkey,), writes=(sqkey, "ln:ssq"))
    E.op("vector", lambda v: v.tensor_scalar_mul(col(MEAN), col(S1), 1.0 / D),
         reads=("ln:s1",), writes=("ln:mean",))
    E.op("vector", lambda v: v.tensor_scalar_mul(col(MSQ), col(SSQ), 1.0 / D),
         reads=("ln:ssq",), writes=("ln:msq",))
    E.op("vector", lambda v: v.tensor_mul(col(M2), col(MEAN), col(MEAN)),
         reads=("ln:mean",), writes=("ln:m2",))
    E.op("vector", lambda v: v.tensor_sub(col(VAR), col(MSQ), col(M2)),
         reads=("ln:msq", "ln:m2"), writes=("ln:var",))
    E.op("vector", lambda v: v.tensor_scalar_add(col(VPE), col(VAR), EPS),
         reads=("ln:var",), writes=("ln:vpe",))
    E.op("scalar", lambda s: s.activation(col(SD), col(VPE), AF.Sqrt),
         reads=("ln:vpe",), writes=("ln:sd",))
    E.op("vector", lambda v: v.reciprocal(col(R0), col(SD)),
         reads=("ln:sd",), writes=("ln:r0",))
    E.op("vector", lambda v: v.tensor_mul(col(RR), col(R0), col(R0)),
         reads=("ln:r0",), writes=("ln:rr",))
    E.op("vector", lambda v: v.tensor_mul(col(H), col(RR), col(VPE)),
         reads=("ln:rr", "ln:vpe"), writes=("ln:h",))
    E.op("vector", lambda v: v.tensor_scalar(col(H2), col(H), -0.5, 1.5,
                                             ALU.mult, ALU.add),
         reads=("ln:h",), writes=("ln:h2",))
    E.op("vector", lambda v: v.tensor_mul(col(RSTD), col(R0), col(H2)),
         reads=("ln:r0", "ln:h2"), writes=("ln:rstd",))
    E.op("vector", lambda v: v.tensor_mul(col(MR), col(MEAN), col(RSTD)),
         reads=("ln:mean", "ln:rstd"), writes=("ln:mr",))
    E.op("vector", lambda v: v.tensor_scalar_mul(col(NMR), col(MR), -1.0),
         reads=("ln:mr",), writes=("ln:nmr",))
    E.op("scalar",
         (lambda s: s.activation(yout[:], z[:], AF.Identity,
                                 bias=col(NMR), scale=col(RSTD))),
         reads=(zkey, "ln:nmr", "ln:rstd"), writes=(ykey,))


_NC_CACHE = None


def _get_nc():
    global _NC_CACHE
    if _NC_CACHE is None:
        _NC_CACHE = build_nc()
    return _NC_CACHE


def kernel(L0_attn_x, L0_attn_Ws, L0_attn_Wa, L0_attn_Wo, L0_attn_bo,
           L0_ffn_W1, L0_ffn_W2, L0_ffn_W3):
    x = np.asarray(L0_attn_x, dtype=np.float32)
    Ws = np.asarray(L0_attn_Ws, dtype=np.float32)
    Wa = np.asarray(L0_attn_Wa, dtype=np.float32)
    Wo = np.asarray(L0_attn_Wo, dtype=np.float32)
    bo = np.asarray(L0_attn_bo, dtype=np.float32).reshape(1, D)
    W1 = np.asarray(L0_ffn_W1, dtype=np.float32)
    W2 = np.asarray(L0_ffn_W2, dtype=np.float32)
    W3 = np.asarray(L0_ffn_W3, dtype=np.float32)

    WoT = np.ascontiguousarray(Wo.T)
    def tile_up(W_):
        t = W_.T.reshape(16, 128, 64, 128)
        return np.ascontiguousarray(t.transpose(2, 1, 0, 3)).reshape(64, 128, 16 * 128)
    w1tp = tile_up(W1)
    w3tp = tile_up(W3)
    w2ch = np.ascontiguousarray(
        W2.reshape(8, 256, 64, 128).transpose(0, 3, 2, 1))

    H128 = _hadamard(128)
    I2H64 = np.kron(np.eye(2, dtype=np.float32), _hadamard(64)).astype(np.float32)
    ident = np.eye(128, dtype=np.float32)
    qi, ki = np.arange(128)[:, None], np.arange(128)[None, :]
    tri = np.where(ki <= qi, 0.0, NEG).astype(np.float32)
    mask_p = [np.concatenate([np.zeros((128, 128), np.float32), tri], axis=1),
              np.concatenate([tri, np.full((128, 128), NEG, np.float32)], axis=1)]

    nc = _get_nc()
    in_maps = []
    perms = []
    for c in range(8):
        b, P = c // 2, c % 2
        tiles = [31 - 2 * j - P for j in range(NSLOT)]
        perm = np.concatenate([np.arange(t * 128, (t + 1) * 128) for t in tiles])
        perms.append((b, perm))
        xb = x[b]
        xbT = np.ascontiguousarray(xb.T)
        in_maps.append({
            "xqT": np.ascontiguousarray(xbT[:, perm]),
            "xq": np.ascontiguousarray(xb[perm]),
            "xkT": xbT,
            "xk": np.ascontiguousarray(xb),
            "ws": Ws, "wa": Wa, "wot": WoT, "bo": bo,
            "w1tp": w1tp, "w3tp": w3tp, "w2ch": w2ch,
            "h128": H128, "i2h64": I2H64, "ident": ident,
            "mask": mask_p[P],
        })

    res = run_bass_kernel_spmd(nc, in_maps, core_ids=list(range(8)))
    out = np.empty((B, L, D), dtype=np.float32)
    for c in range(8):
        b, perm = perms[c]
        out[b][perm] = res.results[c]["out"]
    return out


# version 32 (final)
# speedup vs baseline: 1.0075x; 1.0075x over previous
import numpy as np
from contextlib import ExitStack

import concourse.bass as bass
import concourse.mybir as mybir
from concourse.bass_utils import run_bass_kernel_spmd

F32 = mybir.dt.float32
BF16 = mybir.dt.bfloat16
AF = mybir.ActivationFunctionType
ALU = mybir.AluOpType
AX = mybir.AxisListType

B, L, D, R, HF = 4, 4096, 2048, 256, 8192
TQ = 2048
NSLOT = 16
CJ = [32 - 2 * j for j in range(NSLOT)]
NGRP = 4
GSL = 4
GMAX = [CJ[GSL * g] for g in range(NGRP)]
FT = 1024
NEG = -1.0e30
EPS = 1.0e-5
INV16 = 1.0 / 16.0
FWSCALE = float(1.0 / np.sqrt(HF))

ENGINES = ("tensor", "vector", "scalar", "sync", "gpsimd")


class _Op:
    __slots__ = ("eng", "fn", "deps", "marked", "value", "is_dma", "sem_key",
                 "dma_waits", "snap")

    def __init__(self):
        self.deps = set()
        self.marked = False
        self.value = 0
        self.is_dma = False
        self.sem_key = None
        self.dma_waits = {}


class Emitter:
    def __init__(self):
        self.eng_ops = {e: [] for e in ENGINES}
        self.writer = {}
        self.readers = {}
        self.barrier_ops = []
        self.dma_counts = {}

    def op(self, eng, fn, reads=(), writes=(), dma=None):
        o = _Op()
        o.eng = eng
        o.fn = fn
        if dma is not None:
            o.is_dma = True
            o.sem_key = dma
        deps = o.deps
        for k in reads:
            w = self.writer.get(k)
            if w is not None:
                deps.add(w)
        for k in writes:
            rs = self.readers.get(k)
            if rs:
                deps.update(rs)
            w = self.writer.get(k)
            if w is not None:
                deps.add(w)
        deps.update(self.barrier_ops)
        deps.discard(o)
        if eng == "tensor":
            for p in [p for p in deps if p.eng == "tensor" and not p.is_dma]:
                deps.discard(p)
        for p in list(deps):
            if p.is_dma:
                v = 16 * self.dma_counts[p.sem_key]
                if o.dma_waits.get(p.sem_key, 0) < v:
                    o.dma_waits[p.sem_key] = v
                deps.discard(p)
            else:
                p.marked = True
        if dma is not None:
            self.dma_counts[dma] = self.dma_counts.get(dma, 0) + 1
        for k in reads:
            self.readers.setdefault(k, []).append(o)
        for k in writes:
            self.writer[k] = o
            self.readers[k] = []
        self.eng_ops[eng].append(o)
        return o

    def barrier(self):
        self.barrier_ops = [lst[-1] for lst in self.eng_ops.values() if lst]
        for p in self.barrier_ops:
            if p.is_dma:
                pass
            else:
                p.marked = True

    def finalize(self):
        for e in ("tensor", "vector", "scalar"):
            v = 0
            for o in self.eng_ops[e]:
                if o.marked:
                    v += 1
                    o.value = v

    def replay(self, eng_name, handle, eng_sems, dma_sems):
        waited = {}
        for o in self.eng_ops[eng_name]:
            waits = {}
            for p in o.deps:
                if p.is_dma:
                    continue
                key = ("eng", p.eng)
                if waits.get(key, 0) < p.value:
                    waits[key] = p.value
            for sk, v in o.dma_waits.items():
                key = ("dma", sk)
                if waits.get(key, 0) < v:
                    waits[key] = v
            for key, v in waits.items():
                if waited.get(key, 0) >= v:
                    continue
                waited[key] = v
                sem = eng_sems[key[1]] if key[0] == "eng" else dma_sems[key[1]]
                handle.wait_ge(sem, v)
            ins = o.fn(handle)
            if o.is_dma:
                ins.then_inc(dma_sems[o.sem_key], 16)
            elif o.marked:
                ins.then_inc(eng_sems[o.eng], 1)


def _hadamard(n):
    h = np.array([[1.0]], dtype=np.float32)
    while h.shape[0] < n:
        h = np.block([[h, h], [h, -h]]).astype(np.float32)
    return h


def build_nc():
    nc = bass.Bass("TRN2", target_bir_lowering=False)
    E = Emitter()

    d_xqT = nc.dram_tensor("xqT", [D, TQ], F32, kind="ExternalInput").ap()
    d_xq = nc.dram_tensor("xq", [TQ, D], F32, kind="ExternalInput").ap()
    d_xkT = nc.dram_tensor("xkT", [D, L], F32, kind="ExternalInput").ap()
    d_xk = nc.dram_tensor("xk", [L, D], F32, kind="ExternalInput").ap()
    d_ws = nc.dram_tensor("ws", [D, R], F32, kind="ExternalInput").ap()
    d_wa = nc.dram_tensor("wa", [D, R], F32, kind="ExternalInput").ap()
    d_wot = nc.dram_tensor("wot", [D, D], F32, kind="ExternalInput").ap()
    d_bo = nc.dram_tensor("bo", [1, D], F32, kind="ExternalInput").ap()
    d_w1 = nc.dram_tensor("w1tp", [64, 128, 16 * 128], F32, kind="ExternalInput").ap()
    d_w3 = nc.dram_tensor("w3tp", [64, 128, 16 * 128], F32, kind="ExternalInput").ap()
    d_w2 = nc.dram_tensor("w2ch", [8, 128, 64, 256], F32, kind="ExternalInput").ap()
    d_h128 = nc.dram_tensor("h128", [128, 128], F32, kind="ExternalInput").ap()
    d_i2h64 = nc.dram_tensor("i2h64", [128, 128], F32, kind="ExternalInput").ap()
    d_ident = nc.dram_tensor("ident", [128, 128], F32, kind="ExternalInput").ap()
    d_mask = nc.dram_tensor("mask", [128, 256], F32, kind="ExternalInput").ap()
    d_out = nc.dram_tensor("out", [TQ, D], F32, kind="ExternalOutput").ap()
    d_y1 = nc.dram_tensor("y1d", [TQ, D], F32, kind="Internal").ap()
    d_zz = nc.dram_tensor("zz", [TQ, D], F32, kind="Internal").ap()
    d_vb = nc.dram_tensor("vb", [L, D], BF16, kind="Internal").ap()
    d_hid = nc.dram_tensor("hid", [64, 128, TQ], BF16, kind="Internal").ap()
    d_tmp = nc.dram_tensor("w2tmp", [8, 128, 64, 256], BF16, kind="Internal").ap()
    d_w2p = nc.dram_tensor("w2p", [64, 128, D], BF16, kind="Internal").ap()

    with ExitStack() as S0:
        C = S0.enter_context
        ident_bf = C(nc.sbuf_tensor("ident_bf", [128, 128], BF16))
        h128_bf = C(nc.sbuf_tensor("h128_bf", [128, 128], BF16))
        i2h64_bf = C(nc.sbuf_tensor("i2h64_bf", [128, 128], BF16))
        mask_sb = C(nc.sbuf_tensor("mask_sb", [128, 256], F32))
        bo_bf = C(nc.sbuf_tensor("bo_bf", [1, D], BF16))
        ones1 = C(nc.sbuf_tensor("ones1", [1, 128], BF16))
        lns = C(nc.sbuf_tensor("lns", [128, 32], F32))
        nmax = C(nc.sbuf_tensor("nmax", [128, 1], F32))
        sums = C(nc.sbuf_tensor("sums", [128, 1], F32))
        inv1 = C(nc.sbuf_tensor("inv1", [128, 1], F32))

        E.op("gpsimd", lambda g: g.dma_start(out=ident_bf[:], in_=d_ident),
             writes=("ident_bf",), dma="ident")
        E.op("gpsimd", lambda g: g.dma_start(out=h128_bf[:], in_=d_h128),
             writes=("h128_bf",), dma="h128")
        E.op("gpsimd", lambda g: g.dma_start(out=i2h64_bf[:], in_=d_i2h64),
             writes=("i2h64_bf",), dma="i2h64")
        E.op("gpsimd", lambda g: g.dma_start(out=bo_bf[:], in_=d_bo),
             writes=("bo_bf",), dma="bold")
        E.op("sync", lambda s: s.dma_start(out=mask_sb[:], in_=d_mask),
             writes=("mask_sb",), dma="maskld")
        E.op("vector", lambda v: v.memset(ones1[:], 1.0), writes=("ones1",))

        DC = 256
        NCH = D // DC
        d_tmp_s = d_tmp.rearrange("ch (s l) hi d -> ch s hi l d", s=2)
        d_w2p_s = d_w2p.rearrange("c (s l) t -> s c l t", s=2)
        with ExitStack() as ST:
            CT = ST.enter_context
            rhs1 = [CT(nc.sbuf_tensor(f"rhs1_{i}", [128, 64, DC], BF16)) for i in range(2)]
            t2sb = [CT(nc.sbuf_tensor(f"t2sb_{i}", [128, 64, DC], BF16)) for i in range(1)]
            t1full = [CT(nc.sbuf_tensor(f"t1full_{i}", [128, 64, DC], BF16)) for i in range(1)]
            t2full = [CT(nc.sbuf_tensor(f"t2full_{i}", [128, 64, DC], BF16)) for i in range(1)]
            pt1 = [CT(nc.psum_tensor(f"pt1_{i}", [128, 512], F32)) for i in range(4)]
            pt2 = [CT(nc.psum_tensor(f"pt2_{i}", [128, 512], F32)) for i in range(4)]
            NSC = (64 * DC) // 512

            def emit_load(ch):
                ri = ch % len(rhs1)
                buf = rhs1[ri]
                E.op("gpsimd",
                     (lambda g, ch=ch, buf=buf: g.dma_start(
                         out=buf[:], in_=d_w2[ch])),
                     writes=(f"rhs1:{ri}",), dma=f"w2in{ch % 2}")

            def emit_vb(ch):
                E.op("gpsimd",
                     (lambda g, ch=ch: g.dma_start(
                         out=d_vb[ch * 128:(ch + 1) * 128, :],
                         in_=d_xk[ch * 128:(ch + 1) * 128, :])),
                     writes=(f"vb:{ch}",), dma=f"vbs{ch // 8}")

            def emit_T1(ch):
                ri = ch % len(rhs1)
                buf = rhs1[ri]
                rflat = buf.rearrange("p a b -> p (a b)")
                t1b = t1full[0]
                t1flat = t1b.rearrange("p a b -> p (a b)")
                for c in range(NSC):
                    ps = pt1[c % 4]
                    E.op("tensor",
                         (lambda t, c=c, ps=ps, rflat=rflat:
                          t.matmul(ps[:], h128_bf[:],
                                   rflat[:, c * 512:(c + 1) * 512],
                                   start=True, stop=True,
                                   skip_group_check=True)),
                         reads=(f"rhs1:{ri}", "h128_bf"),
                         writes=(f"pt1{c % 4}",))
                    if c % 2 == 0:
                        E.op("vector",
                             (lambda h, c=c, ps=ps, t1flat=t1flat: h.tensor_scalar(
                                 t1flat[:, c * 512:(c + 1) * 512],
                                 ps[:], FWSCALE, None, ALU.mult)),
                             reads=(f"pt1{c % 4}",), writes=(f"t1f:0:{c}",))
                    else:
                        E.op("scalar",
                             (lambda h, c=c, ps=ps, t1flat=t1flat: h.activation(
                                 t1flat[:, c * 512:(c + 1) * 512],
                                 ps[:], AF.Copy, scale=FWSCALE)),
                             reads=(f"pt1{c % 4}",), writes=(f"t1f:0:{c}",))
                E.op("sync",
                     (lambda s, ch=ch, t1b=t1b: s.dma_start(
                         out=d_tmp[ch], in_=t1b[:])),
                     reads=tuple(f"t1f:0:{c}" for c in range(NSC)),
                     writes=(f"tmpD:{ch}",), dma=f"tmpw{ch % 2}")

            def emit_T2(ch):
                d0 = ch * DC
                buf = t2sb[0]
                rflat = buf.rearrange("p a b -> p (a b)")
                t2b = t2full[0]
                t2flat = t2b.rearrange("p a b -> p (a b)")
                for sh in range(2):
                    E.op("sync",
                         (lambda s, ch=ch, buf=buf, sh=sh: s.dma_start(
                             out=buf[sh * 64:(sh + 1) * 64, :, :],
                             in_=d_tmp_s[ch, sh])),
                         reads=(f"tmpD:{ch}",),
                         writes=("t2sb:0",), dma=f"tmpr{ch % 2}")
                for c in range(NSC):
                    ps = pt2[c % 4]
                    E.op("tensor",
                         (lambda t, c=c, ps=ps, rflat=rflat:
                          t.matmul(ps[:], i2h64_bf[:],
                                   rflat[:, c * 512:(c + 1) * 512],
                                   start=True, stop=True,
                                   skip_group_check=True)),
                         reads=("t2sb:0", "i2h64_bf"),
                         writes=(f"pt2{c % 4}",))
                    if c % 2 == 0:
                        E.op("vector",
                             (lambda h, c=c, ps=ps, t2flat=t2flat: h.tensor_copy(
                                 t2flat[:, c * 512:(c + 1) * 512], ps[:])),
                             reads=(f"pt2{c % 4}",), writes=(f"t2f:0:{c}",))
                    else:
                        E.op("scalar",
                             (lambda h, c=c, ps=ps, t2flat=t2flat: h.activation(
                                 t2flat[:, c * 512:(c + 1) * 512], ps[:], AF.Copy)),
                             reads=(f"pt2{c % 4}",), writes=(f"t2f:0:{c}",))
                for sh in range(2):
                    E.op("sync",
                         (lambda s, ch=ch, d0=d0, sh=sh, t2b=t2b: s.dma_start(
                             out=d_w2p_s[sh, :, :, d0:d0 + DC],
                             in_=t2b[sh * 64:(sh + 1) * 64, :, :])),
                         reads=tuple(f"t2f:0:{c}" for c in range(NSC)),
                         writes=(f"w2p:{ch}",), dma="w2pw")

            emit_load(0)
            emit_load(1)
            for ch in range(NCH):
                emit_T1(ch)
                if ch + 2 < NCH:
                    emit_load(ch + 2)
                for q in range(4):
                    emit_vb(4 * ch + q)
                if ch >= 2:
                    emit_T2(ch - 2)
            emit_T2(NCH - 2)
            emit_T2(NCH - 1)
            E.barrier()

        SA = ExitStack()
        CA = SA.enter_context
        st_bf = [CA(nc.sbuf_tensor(f"st{r}", [128, TQ], BF16)) for r in range(2)]
        ac_bf = [CA(nc.sbuf_tensor(f"ac{r}", [128, L], BF16)) for r in range(2)]
        with ExitStack() as S1:
            C1 = S1.enter_context
            ws_bf = C1(nc.sbuf_tensor("ws_bf", [128, 16 * R], BF16))
            wa_bf = C1(nc.sbuf_tensor("wa_bf", [128, 16 * R], BF16))
            xsb = [C1(nc.sbuf_tensor(f"xsb{i}", [128, 2048], BF16)) for i in range(4)]
            pj = [C1(nc.psum_tensor(f"pj{i}", [128, 512], F32)) for i in range(8)]

            for k in range(16):
                E.op("gpsimd",
                     (lambda g, k=k: g.dma_start(
                         out=ws_bf[:, k * R:(k + 1) * R],
                         in_=d_ws[k * 128:(k + 1) * 128, :])),
                     writes=(f"ws:{k}",), dma="wsld")
                E.op("gpsimd",
                     (lambda g, k=k: g.dma_start(
                         out=wa_bf[:, k * R:(k + 1) * R],
                         in_=d_wa[k * 128:(k + 1) * 128, :])),
                     writes=(f"wa:{k}",), dma="wald")

            def proj(which, wbuf, wkey, src, src_col0, dst, dst_col0):
                for k in range(16):
                    buf = xsb[k % 4]
                    E.op("gpsimd",
                         (lambda g, k=k, buf=buf: g.dma_start(
                             out=buf[:],
                             in_=src[k * 128:(k + 1) * 128,
                                     src_col0:src_col0 + 2048])),
                         reads=(), writes=(f"xsb{k % 4}",), dma=f"xsb{k % 4}")
                    for m in range(2):
                        for n in range(4):
                            p = pj[m * 4 + n]
                            E.op("tensor",
                                 (lambda t, k=k, m=m, n=n, p=p, buf=buf:
                                  t.matmul(p[:],
                                           wbuf[:, k * R + m * 128:k * R + (m + 1) * 128],
                                           buf[:, n * 512:(n + 1) * 512],
                                           start=(k == 0), stop=(k == 15),
                                           skip_group_check=True)),
                                 reads=(f"{wkey}:{k}", f"xsb{k % 4}"),
                                 writes=(f"pj{m * 4 + n}",))
                for m in range(2):
                    for n in range(4):
                        p = pj[m * 4 + n]
                        scale = INV16 if which == "state" else None
                        if scale is not None:
                            E.op("vector",
                                 (lambda v, m=m, n=n, p=p:
                                  v.tensor_scalar(
                                      dst[m][:, dst_col0 + n * 512:dst_col0 + (n + 1) * 512],
                                      p[:], INV16, None, ALU.mult)),
                                 reads=(f"pj{m * 4 + n}",),
                                 writes=(f"{'st' if which == 'state' else 'ac'}:{m}:{dst_col0 + n * 512}",))
                        else:
                            E.op("vector",
                                 (lambda v, m=m, n=n, p=p:
                                  v.tensor_copy(
                                      dst[m][:, dst_col0 + n * 512:dst_col0 + (n + 1) * 512],
                                      p[:])),
                                 reads=(f"pj{m * 4 + n}",),
                                 writes=(f"ac:{m}:{dst_col0 + n * 512}",))

            proj("state", ws_bf, "ws", d_xqT, 0, st_bf, 0)
            proj("action", wa_bf, "wa", d_xkT, 0, ac_bf, 0)
            proj("action", wa_bf, "wa", d_xkT, 2048, ac_bf, 2048)
            E.barrier()

        with ExitStack() as S2:
            C2 = S2.enter_context
            wot_bf = C2(nc.sbuf_tensor("wot_bf", [128, 16 * D], BF16))
            Praw = C2(nc.sbuf_tensor("Praw", [128, 4096], F32))
            Pbf2 = [C2(nc.sbuf_tensor(f"Pbf{i}", [128, 4096], BF16)) for i in range(2)]
            PT2 = [C2(nc.sbuf_tensor(f"PT{i}", [128, 32 * 256], BF16)) for i in range(2)]
            atT = C2(nc.sbuf_tensor("atT", [128, 16 * 256], BF16))
            v_c = [C2(nc.sbuf_tensor(f"vc{i}", [128, 1024], BF16)) for i in range(4)]
            z1b = [C2(nc.sbuf_tensor(f"z1b{i}", [128, D], F32)) for i in range(2)]
            y1b = C2(nc.sbuf_tensor("y1b", [128, D], F32))
            xqt = C2(nc.sbuf_tensor("xqt", [128, D], F32))
            ptps = [C2(nc.psum_tensor(f"ptps{i}", [128, 128], BF16)) for i in range(2)]
            sc = [C2(nc.psum_tensor(f"sc{i}", [128, 512], F32)) for i in range(2)]
            av = [C2(nc.psum_tensor(f"av{i}", [128, 512], F32)) for i in range(4)]

            for k in range(16):
                E.op("gpsimd",
                     (lambda g, k=k: g.dma_start(
                         out=wot_bf[:, k * D:(k + 1) * D],
                         in_=d_wot[k * 128:(k + 1) * 128, :])),
                     writes=(f"wot:{k}",), dma="wotld")

            def emit_scores(p, i):
                j = 2 * p + i
                cj = CJ[j]
                W = cj * 128
                Pbf = Pbf2[i]
                pbk = f"pbf{i}"
                off = 0
                bi = 0
                while off < W:
                    wdt = min(512, W - off)
                    ps = sc[bi % 2]
                    for r in range(2):
                        E.op("tensor",
                             (lambda t, r=r, j=j, off=off, wdt=wdt, ps=ps:
                              t.matmul(ps[:, 0:wdt],
                                       st_bf[r][:, j * 128:(j + 1) * 128],
                                       ac_bf[r][:, off:off + wdt],
                                       start=(r == 0), stop=(r == 1))),
                             reads=(f"st:{r}:{(j * 128) // 512 * 512}",
                                    f"ac:{r}:{(off // 512) * 512}",
                                    f"ac:{r}:{((off + wdt - 1) // 512) * 512}"),
                             writes=(f"scp{bi % 2}",))
                    if off + wdt == W:
                        if wdt == 512:
                            E.op("vector",
                                 (lambda v, off=off, ps=ps:
                                  v.tensor_copy(Praw[:, off:off + 256], ps[:, 0:256])),
                                 reads=(f"scp{bi % 2}",), writes=("praw",))
                            E.op("vector",
                                 (lambda v, off=off, ps=ps:
                                  v.tensor_tensor(Praw[:, off + 256:off + 512],
                                                  ps[:, 256:512], mask_sb[:], ALU.add)),
                                 reads=(f"scp{bi % 2}", "mask_sb"), writes=("praw",))
                        else:
                            E.op("vector",
                                 (lambda v, off=off, ps=ps:
                                  v.tensor_tensor(Praw[:, off:off + 256],
                                                  ps[:, 0:256], mask_sb[:], ALU.add)),
                                 reads=(f"scp{bi % 2}", "mask_sb"), writes=("praw",))
                    else:
                        E.op("vector",
                             (lambda v, off=off, wdt=wdt, ps=ps:
                              v.tensor_copy(Praw[:, off:off + wdt], ps[:, 0:wdt])),
                             reads=(f"scp{bi % 2}",), writes=("praw",))
                    off += wdt
                    bi += 1
                E.op("vector",
                     (lambda v, W=W: v.reduce_max(nmax[:], Praw[:, 0:W],
                                                  axis=AX.X, negate=True)),
                     reads=("praw",), writes=("nmax",))
                E.op("scalar",
                     (lambda s, W=W, Pbf=Pbf: s.activation(
                         Pbf[:, 0:W], Praw[:, 0:W], AF.Exp, bias=nmax[:],
                         accum_out=sums[:])),
                     reads=("praw", "nmax"), writes=(pbk, "sums"))
                E.op("vector", lambda v: v.reciprocal(inv1[:], sums[:]),
                     reads=("sums",), writes=("inv1",))
                E.op("vector",
                     (lambda v, W=W, Pbf=Pbf: v.tensor_scalar_mul(
                         Pbf[:, 0:W], Pbf[:, 0:W], inv1[:])),
                     reads=(pbk, "inv1"), writes=(pbk,))

            def emit_av_memsets(p, half):
                for b in range(4):
                    E.op("vector",
                         (lambda v, b=b: v.memset(av[b][:], 0.0)),
                         writes=(f"av{b}",))

            def emit_av_half(p, half, inter=None):
                maxp = CJ[2 * p]
                for ch in range(maxp):
                    buf = v_c[ch % 4]
                    E.op("sync",
                         (lambda g2, ch=ch, half=half, buf=buf: g2.dma_start(
                             out=buf[:],
                             in_=d_vb[ch * 128:(ch + 1) * 128,
                                      half * 1024:(half + 1) * 1024])),
                         reads=(f"vb:{ch}",),
                         writes=(f"vc{ch % 4}",), dma=f"vc{ch % 4}")
                    for m in range(8):
                        E.op("tensor",
                             (lambda t, ch=ch, m=m, maxp=maxp, buf=buf, p=p:
                              t.matmul(av[m // 2][:, (m % 2) * 256:(m % 2) * 256 + 256],
                                       buf[:, m * 128:(m + 1) * 128],
                                       PT2[p % 2][:, ch * 256:(ch + 1) * 256],
                                       start=False, stop=(ch == maxp - 1),
                                       skip_group_check=True)),
                             reads=(f"vc{ch % 4}", f"pt:{ch}:0", f"pt:{ch}:1"),
                             writes=(f"av{m // 2}",))
                    if inter:
                        n_per = max(1, len(inter) // maxp + 1)
                        for _ in range(n_per):
                            if inter:
                                inter.pop(0)()
                for m in range(8):
                    hc = half * 8 + m
                    E.op("vector",
                         (lambda v, hc=hc, m=m:
                          v.tensor_copy(atT[:, hc * 256:(hc + 1) * 256],
                                        av[m // 2][:, (m % 2) * 256:(m % 2) * 256 + 256])),
                         reads=(f"av{m // 2}",), writes=(f"atT:{hc}",))
                if inter:
                    while inter:
                        inter.pop(0)()

            def pt_thunks(p):
                maxp = CJ[2 * p]
                thunks = []
                for i in range(2):
                    cj = CJ[2 * p + i]
                    Pbf = Pbf2[i]
                    pbk = f"pbf{i}"
                    for kb in range(cj):
                        pp = ptps[(len(thunks)) % 2]
                        def th(kb=kb, i=i, pp=pp, Pbf=Pbf, pbk=pbk, p=p,
                               ppk=f"ptps{len(thunks) % 2}"):
                            E.op("tensor",
                                 (lambda t: t.transpose(
                                     pp[:], Pbf[:, kb * 128:(kb + 1) * 128],
                                     ident_bf[:])),
                                 reads=(pbk, "ident_bf"), writes=(ppk,))
                            E.op("vector",
                                 (lambda v: v.tensor_copy(
                                     PT2[p % 2][:, kb * 256 + i * 128:kb * 256 + (i + 1) * 128],
                                     pp[:])),
                                 reads=(ppk,), writes=(f"pt:{kb}:{i}",))
                        thunks.append(th)
                    for kb in range(cj, maxp):
                        def thz(kb=kb, i=i, p=p):
                            E.op("vector",
                                 (lambda v: v.memset(
                                     PT2[p % 2][:, kb * 256 + i * 128:kb * 256 + (i + 1) * 128], 0.0)),
                                 writes=(f"pt:{kb}:{i}",))
                        thunks.append(thz)
                return thunks

            def emit_wo_epi(p):
                for i in range(2):
                    j = 2 * p + i
                    E.op("sync",
                         (lambda s, j=j: s.dma_start(
                             out=xqt[:], in_=d_xq[j * 128:(j + 1) * 128, :])),
                         writes=("xqt",), dma="xqtld")
                    z1 = z1b[i]
                    zkey = f"z1b{i}"
                    for n in range(4):
                        ps = sc[n % 2]
                        for k in range(16):
                            E.op("tensor",
                                 (lambda t, k=k, n=n, i=i, ps=ps:
                                  t.matmul(ps[:],
                                           atT[:, k * 256 + i * 128:k * 256 + (i + 1) * 128],
                                           wot_bf[:, k * D + n * 512:k * D + (n + 1) * 512],
                                           start=(k == 0), stop=False,
                                           skip_group_check=True)),
                                 reads=(f"atT:{k}", f"wot:{k}"),
                                 writes=(f"scp{n % 2}",))
                        E.op("tensor",
                             (lambda t, n=n, ps=ps:
                              t.matmul(ps[:], ones1[:],
                                       bo_bf[:, n * 512:(n + 1) * 512],
                                       start=False, stop=True,
                                       skip_group_check=True)),
                             reads=("ones1", "bo_bf"), writes=(f"scp{n % 2}",))
                        E.op("vector",
                             (lambda v, n=n, ps=ps, z1=z1:
                              v.tensor_tensor(z1[:, n * 512:(n + 1) * 512], ps[:],
                                              xqt[:, n * 512:(n + 1) * 512], ALU.add)),
                             reads=(f"scp{n % 2}", "xqt"), writes=(zkey,))
                _ln_pair(E, z1b, ("z1b0", "z1b1"), y1b, lns, Praw, d_y1, p)

            order = [7, 6, 5, 4, 3, 2, 1, 0]
            for step in range(9):
                prev = order[step - 1] if step >= 1 else None
                cur = order[step] if step < 8 else None
                if prev is not None:
                    emit_av_memsets(prev, 0)
                if cur is not None:
                    emit_scores(cur, 0)
                    emit_scores(cur, 1)
                if prev is not None:
                    emit_av_half(prev, 0)
                    emit_av_memsets(prev, 1)
                    emit_av_half(prev, 1,
                                 inter=(pt_thunks(cur) if cur is not None else None))
                    emit_wo_epi(prev)
                elif cur is not None:
                    for th in pt_thunks(cur):
                        th()
            E.barrier()
        SA.close()

        with ExitStack() as SB1:
            CB = SB1.enter_context
            y1t = CB(nc.sbuf_tensor("y1t", [128, 16 * TQ], BF16))
            w1b = [CB(nc.sbuf_tensor(f"w1b{i}", [128, 16 * 128], BF16)) for i in range(3)]
            w3b = [CB(nc.sbuf_tensor(f"w3b{i}", [128, 16 * 128], BF16)) for i in range(3)]
            y1bf = [CB(nc.sbuf_tensor(f"y1bf{i}", [128, D], BF16)) for i in range(2)]
            hstg = [CB(nc.sbuf_tensor(f"hstg{i}", [128, 512], BF16)) for i in range(4)]
            silu = [CB(nc.sbuf_tensor(f"silu{i}", [128, 512], BF16)) for i in range(2)]
            trps = CB(nc.psum_tensor("trps", [128, 128], BF16))
            up = [CB(nc.psum_tensor(f"up{i}", [128, 512], F32)) for i in range(4)]

            for m in range(16):
                buf = y1bf[m % 2]
                E.op("gpsimd",
                     (lambda g2, m=m, buf=buf: g2.dma_start(
                         out=buf[:],
                         in_=d_y1[m * 128:(m + 1) * 128, :])),
                     reads=(f"y1d:{m}",),
                     writes=(f"y1bf{m % 2}",), dma=f"y1bf{m % 2}")
                for c in range(16):
                    E.op("tensor",
                         (lambda t, c=c, buf=buf, m=m:
                          t.transpose(trps[:], buf[:, c * 128:(c + 1) * 128],
                                      ident_bf[:])),
                         reads=(f"y1bf{m % 2}", "ident_bf"), writes=("trps",))
                    E.op("vector",
                         (lambda v, c=c, m=m:
                          v.tensor_copy(
                              y1t[:, c * TQ + m * 128:c * TQ + (m + 1) * 128],
                              trps[:])),
                         reads=("trps",), writes=(f"y1t:{c}:{m}",))
            for hc in range(64):
                bw1, bw3 = w1b[hc % 3], w3b[hc % 3]
                E.op("gpsimd",
                     (lambda g2, hc=hc, bw1=bw1: g2.dma_start(
                         out=bw1[:], in_=d_w1[hc])),
                     writes=(f"w1b{hc % 3}",), dma=f"w1b{hc % 3}")
                E.op("gpsimd",
                     (lambda g2, hc=hc, bw3=bw3: g2.dma_start(
                         out=bw3[:], in_=d_w3[hc])),
                     writes=(f"w3b{hc % 3}",), dma=f"w3b{hc % 3}")
                for n in range(4):
                    p1 = up[n % 2]
                    p3 = up[2 + n % 2]
                    for k in range(16):
                        E.op("tensor",
                             (lambda t, k=k, n=n, p1=p1, bw1=bw1:
                              t.matmul(p1[:],
                                       bw1[:, k * 128:(k + 1) * 128],
                                       y1t[:, k * TQ + n * 512:k * TQ + n * 512 + 512],
                                       start=(k == 0), stop=(k == 15),
                                       skip_group_check=True)),
                             reads=(f"w1b{hc % 3}",) + tuple(
                                 f"y1t:{k}:{n * 4 + mm}" for mm in range(4)),
                             writes=(f"up{n % 2}",))
                    for k in range(16):
                        E.op("tensor",
                             (lambda t, k=k, n=n, p3=p3, bw3=bw3:
                              t.matmul(p3[:],
                                       bw3[:, k * 128:(k + 1) * 128],
                                       y1t[:, k * TQ + n * 512:k * TQ + n * 512 + 512],
                                       start=(k == 0), stop=(k == 15),
                                       skip_group_check=True)),
                             reads=(f"w3b{hc % 3}",) + tuple(
                                 f"y1t:{k}:{n * 4 + mm}" for mm in range(4)),
                             writes=(f"up{2 + n % 2}",))
                    sl = silu[n % 2]
                    hb = hstg[n]
                    E.op("scalar",
                         (lambda s, p1=p1, sl=sl: s.activation(sl[:], p1[:], AF.Silu)),
                         reads=(f"up{n % 2}",), writes=(f"silu{n % 2}",))
                    E.op("vector",
                         (lambda v, p3=p3, sl=sl, hb=hb:
                          v.tensor_tensor(hb[:], sl[:], p3[:], ALU.mult)),
                         reads=(f"silu{n % 2}", f"up{2 + n % 2}"),
                         writes=(f"hstg{n}",))
                    E.op("sync",
                         (lambda s2, hc=hc, n=n, hb=hb: s2.dma_start(
                             out=d_hid[hc][:, n * 512:(n + 1) * 512], in_=hb[:])),
                         reads=(f"hstg{n}",), writes=(f"hid:{hc}:{n}",), dma="hidw")
            E.barrier()

        FT2 = 512
        with ExitStack() as SB2:
            CB2 = SB2.enter_context
            Hd2r = [CB2(nc.sbuf_tensor(f"Hd2_{i}", [128, 64 * FT2], BF16)) for i in range(2)]
            zt = CB2(nc.sbuf_tensor("zt", [128, D], F32))
            yt = CB2(nc.sbuf_tensor("yt", [128, D], F32))
            ot = CB2(nc.sbuf_tensor("ot", [128, D], F32))
            sqc = CB2(nc.sbuf_tensor("sqc", [128, D], BF16))
            w2h = [CB2(nc.sbuf_tensor(f"w2h{i}", [128, 8 * 512], BF16)) for i in range(4)]
            z2s = [CB2(nc.sbuf_tensor(f"z2s{i}", [128, 512], F32)) for i in range(2)]
            dps = [CB2(nc.psum_tensor(f"dps{i}", [128, 512], F32)) for i in range(8)]

            def emit_C(tc):
                for mc in range(tc * 4, tc * 4 + 4):
                    E.op("sync",
                         (lambda s, mc=mc: s.dma_start(
                             out=zt[:], in_=d_zz[mc * 128:(mc + 1) * 128, :])),
                         reads=tuple(f"zz:{mc // 4}:{mc % 4}:{n}" for n in range(4)),
                         writes=("zt",), dma="ztld")
                    E.op("gpsimd",
                         (lambda s, mc=mc: s.dma_start(
                             out=yt[:], in_=d_y1[mc * 128:(mc + 1) * 128, :])),
                         reads=(f"y1d:{mc}",), writes=("yt",), dma="ytld")
                    E.op("vector",
                         (lambda v: v.tensor_add(zt[:], zt[:], yt[:])),
                         reads=("zt", "yt"), writes=("zt",))
                    _ln_norm(E, nc, zt, "zt", ot, "ot", lns, sqc, "sqc")
                    E.op("sync",
                         (lambda s, mc=mc: s.dma_start(
                             out=d_out[mc * 128:(mc + 1) * 128, :], in_=ot[:])),
                         reads=("ot",), writes=(f"out:{mc}",), dma="outw")

            d_hid_r = d_hid.rearrange("c p t -> p c t")
            d_w2p_r = d_w2p.rearrange("c p t -> p c t")
            def emit_h2l(tt):
                Hd2 = Hd2r[tt % 2]
                for hc0 in range(0, 64, 8):
                    rd = tuple(f"hid:{hc}:{tt}" for hc in range(hc0, hc0 + 8))
                    E.op("gpsimd",
                         (lambda s2, hc0=hc0, tt=tt, Hd2=Hd2: s2.dma_start(
                             out=Hd2[:, hc0 * FT2:(hc0 + 8) * FT2],
                             in_=d_hid_r[:, hc0:hc0 + 8,
                                         tt * FT2:(tt + 1) * FT2])),
                         reads=rd, writes=(f"Hd2:{tt % 2}:{hc0}",), dma="h2l")

            emit_h2l(0)
            for tt in range(4):
                if tt + 1 < 4:
                    emit_h2l(tt + 1)
                if tt >= 1:
                    emit_C(tt - 1)
                Hd2 = Hd2r[tt % 2]
                hkey = f"Hd2:{tt % 2}"
                dbase = 4 * (tt % 2)
                for n in range(4):
                    for qb in range(8):
                        s_idx = n * 8 + qb
                        buf = w2h[s_idx % 4]
                        hh0 = qb * 8
                        E.op("gpsimd",
                             (lambda g2, hh0=hh0, n=n, buf=buf: g2.dma_start(
                                 out=buf[:],
                                 in_=d_w2p_r[:, hh0:hh0 + 8,
                                             n * 512:(n + 1) * 512])),
                             reads=tuple(f"w2p:{ch}"
                                         for ch in range(2 * n, 2 * n + 2)),
                             writes=(f"w2q{s_idx % 4}",),
                             dma=f"w2h{s_idx % 4}")
                        for m in range(4):
                            for h2 in range(8):
                                hh = hh0 + h2
                                E.op("tensor",
                                     (lambda t, m=m, h2=h2, hh=hh, qb=qb,
                                      buf=buf, Hd2=Hd2, dbase=dbase:
                                      t.matmul(dps[dbase + m][:],
                                               Hd2[:, hh * FT2 + m * 128:hh * FT2 + (m + 1) * 128],
                                               buf[:, h2 * 512:(h2 + 1) * 512],
                                               start=(qb == 0 and h2 == 0),
                                               stop=(qb == 7 and h2 == 7),
                                               skip_group_check=True)),
                                     reads=(f"w2q{s_idx % 4}", f"{hkey}:{hh0}"),
                                     writes=(f"dps{dbase + m}",))
                    for m in range(4):
                        zb = z2s[m % 2]
                        E.op("vector",
                             (lambda v, m=m, zb=zb, dbase=dbase:
                              v.tensor_copy(zb[:], dps[dbase + m][:])),
                             reads=(f"dps{dbase + m}",), writes=(f"z2s{m % 2}",))
                        E.op("sync",
                             (lambda s, tt=tt, m=m, n=n, zb=zb: s.dma_start(
                                 out=d_zz[tt * FT2 + m * 128:tt * FT2 + (m + 1) * 128,
                                          n * 512:(n + 1) * 512],
                                 in_=zb[:])),
                             reads=(f"z2s{m % 2}",), writes=(f"zz:{tt}:{m}:{n}",),
                             dma="zzw")
            emit_C(3)

        E.finalize()

        eng_sems = {}
        dma_sems = {}
        for e in ("tensor", "vector", "scalar"):
            eng_sems[e] = C(nc.semaphore(f"sem_{e}"))
        for sk in E.dma_counts:
            dma_sems[sk] = C(nc.semaphore(f"ds_{sk}"))

        block = C(nc.Block())
        n_out = E.dma_counts["outw"]

        @block.tensor
        def _(t):
            E.replay("tensor", t, eng_sems, dma_sems)

        @block.vector
        def _(v):
            E.replay("vector", v, eng_sems, dma_sems)

        @block.scalar
        def _(s):
            E.replay("scalar", s, eng_sems, dma_sems)

        @block.gpsimd
        def _(g):
            E.replay("gpsimd", g, eng_sems, dma_sems)

        @block.sync
        def _(s):
            E.replay("sync", s, eng_sems, dma_sems)
            s.wait_ge(dma_sems["outw"], 16 * n_out)

    return nc


def _ln_pair(E, z1b, zkeys, y1b, lns, praw, d_y1, p):
    S1, SSQ, MEAN, MSQ, M2, VAR, VPE, SD, R0, RR, H, H2, RSTD, MR, NMR = range(15)

    def col(c, i):
        return lns[:, c + 16 * i:c + 16 * i + 1]

    def vop(i, fn, rd, wr):
        E.op("vector", fn, reads=rd, writes=wr)

    for i in range(2):
        z, zk = z1b[i], zkeys[i]
        E.op("vector", (lambda v, z=z, i=i: v.reduce_sum(col(S1, i), z[:], axis=AX.X)),
             reads=(zk,), writes=(f"ln{i}:s1",))
    for i in range(2):
        z, zk = z1b[i], zkeys[i]
        E.op("scalar",
             (lambda s, z=z, i=i: s.activation(praw[:, 2048:2048 + D], z[:], AF.Square,
                                               accum_out=col(SSQ, i))),
             reads=(zk,), writes=("praw", f"ln{i}:ssq"))
    for i in range(2):
        E.op("vector", (lambda v, i=i: v.tensor_scalar_mul(col(MEAN, i), col(S1, i), 1.0 / D)),
             reads=(f"ln{i}:s1",), writes=(f"ln{i}:mean",))
        E.op("vector", (lambda v, i=i: v.tensor_scalar_mul(col(MSQ, i), col(SSQ, i), 1.0 / D)),
             reads=(f"ln{i}:ssq",), writes=(f"ln{i}:msq",))
        E.op("vector", (lambda v, i=i: v.tensor_mul(col(M2, i), col(MEAN, i), col(MEAN, i))),
             reads=(f"ln{i}:mean",), writes=(f"ln{i}:m2",))
        E.op("vector", (lambda v, i=i: v.tensor_sub(col(VAR, i), col(MSQ, i), col(M2, i))),
             reads=(f"ln{i}:msq", f"ln{i}:m2"), writes=(f"ln{i}:var",))
        E.op("vector", (lambda v, i=i: v.tensor_scalar_add(col(VPE, i), col(VAR, i), EPS)),
             reads=(f"ln{i}:var",), writes=(f"ln{i}:vpe",))
    for i in range(2):
        E.op("scalar", (lambda s, i=i: s.activation(col(SD, i), col(VPE, i), AF.Sqrt)),
             reads=(f"ln{i}:vpe",), writes=(f"ln{i}:sd",))
    for i in range(2):
        E.op("vector", (lambda v, i=i: v.reciprocal(col(R0, i), col(SD, i))),
             reads=(f"ln{i}:sd",), writes=(f"ln{i}:r0",))
        E.op("vector", (lambda v, i=i: v.tensor_mul(col(RR, i), col(R0, i), col(R0, i))),
             reads=(f"ln{i}:r0",), writes=(f"ln{i}:rr",))
        E.op("vector", (lambda v, i=i: v.tensor_mul(col(H, i), col(RR, i), col(VPE, i))),
             reads=(f"ln{i}:rr", f"ln{i}:vpe"), writes=(f"ln{i}:h",))
        E.op("vector", (lambda v, i=i: v.tensor_scalar(col(H2, i), col(H, i), -0.5, 1.5,
                                                       ALU.mult, ALU.add)),
             reads=(f"ln{i}:h",), writes=(f"ln{i}:h2",))
        E.op("vector", (lambda v, i=i: v.tensor_mul(col(RSTD, i), col(R0, i), col(H2, i))),
             reads=(f"ln{i}:r0", f"ln{i}:h2"), writes=(f"ln{i}:rstd",))
        E.op("vector", (lambda v, i=i: v.tensor_mul(col(MR, i), col(MEAN, i), col(RSTD, i))),
             reads=(f"ln{i}:mean", f"ln{i}:rstd"), writes=(f"ln{i}:mr",))
        E.op("vector", (lambda v, i=i: v.tensor_scalar_mul(col(NMR, i), col(MR, i), -1.0)),
             reads=(f"ln{i}:mr",), writes=(f"ln{i}:nmr",))
    for i in range(2):
        j = 2 * p + i
        z, zk = z1b[i], zkeys[i]
        E.op("scalar",
             (lambda s, z=z, i=i: s.activation(y1b[:], z[:], AF.Identity,
                                               bias=col(NMR, i), scale=col(RSTD, i))),
             reads=(zk, f"ln{i}:nmr", f"ln{i}:rstd"), writes=("y1b",))
        E.op("sync",
             (lambda s, j=j: s.dma_start(
                 out=d_y1[j * 128:(j + 1) * 128, :], in_=y1b[:])),
             reads=("y1b",), writes=(f"y1d:{j}",), dma="y1w")


def _ln_norm(E, nc, z, zkey, yout, ykey, lns, sq_scratch, sqkey):
    S1, SSQ, MEAN, MSQ, M2, VAR, VPE, SD, R0, RR, H, H2, RSTD, MR, NMR = range(15)

    def col(c):
        return lns[:, c:c + 1]

    E.op("vector", lambda v: v.reduce_sum(col(S1), z[:], axis=AX.X),
         reads=(zkey,), writes=("ln:s1",))
    E.op("scalar",
         (lambda s: s.activation(sq_scratch[:, 0:D], z[:], AF.Square,
                                 accum_out=col(SSQ))),
         reads=(zkey,), writes=(sqkey, "ln:ssq"))
    E.op("vector", lambda v: v.tensor_scalar_mul(col(MEAN), col(S1), 1.0 / D),
         reads=("ln:s1",), writes=("ln:mean",))
    E.op("vector", lambda v: v.tensor_scalar_mul(col(MSQ), col(SSQ), 1.0 / D),
         reads=("ln:ssq",), writes=("ln:msq",))
    E.op("vector", lambda v: v.tensor_mul(col(M2), col(MEAN), col(MEAN)),
         reads=("ln:mean",), writes=("ln:m2",))
    E.op("vector", lambda v: v.tensor_sub(col(VAR), col(MSQ), col(M2)),
         reads=("ln:msq", "ln:m2"), writes=("ln:var",))
    E.op("vector", lambda v: v.tensor_scalar_add(col(VPE), col(VAR), EPS),
         reads=("ln:var",), writes=("ln:vpe",))
    E.op("scalar", lambda s: s.activation(col(SD), col(VPE), AF.Sqrt),
         reads=("ln:vpe",), writes=("ln:sd",))
    E.op("vector", lambda v: v.reciprocal(col(R0), col(SD)),
         reads=("ln:sd",), writes=("ln:r0",))
    E.op("vector", lambda v: v.tensor_mul(col(RR), col(R0), col(R0)),
         reads=("ln:r0",), writes=("ln:rr",))
    E.op("vector", lambda v: v.tensor_mul(col(H), col(RR), col(VPE)),
         reads=("ln:rr", "ln:vpe"), writes=("ln:h",))
    E.op("vector", lambda v: v.tensor_scalar(col(H2), col(H), -0.5, 1.5,
                                             ALU.mult, ALU.add),
         reads=("ln:h",), writes=("ln:h2",))
    E.op("vector", lambda v: v.tensor_mul(col(RSTD), col(R0), col(H2)),
         reads=("ln:r0", "ln:h2"), writes=("ln:rstd",))
    E.op("vector", lambda v: v.tensor_mul(col(MR), col(MEAN), col(RSTD)),
         reads=("ln:mean", "ln:rstd"), writes=("ln:mr",))
    E.op("vector", lambda v: v.tensor_scalar_mul(col(NMR), col(MR), -1.0),
         reads=("ln:mr",), writes=("ln:nmr",))
    E.op("scalar",
         (lambda s: s.activation(yout[:], z[:], AF.Identity,
                                 bias=col(NMR), scale=col(RSTD))),
         reads=(zkey, "ln:nmr", "ln:rstd"), writes=(ykey,))


_NC_CACHE = None


def _get_nc():
    global _NC_CACHE
    if _NC_CACHE is None:
        _NC_CACHE = build_nc()
    return _NC_CACHE


def kernel(L0_attn_x, L0_attn_Ws, L0_attn_Wa, L0_attn_Wo, L0_attn_bo,
           L0_ffn_W1, L0_ffn_W2, L0_ffn_W3):
    x = np.asarray(L0_attn_x, dtype=np.float32)
    Ws = np.asarray(L0_attn_Ws, dtype=np.float32)
    Wa = np.asarray(L0_attn_Wa, dtype=np.float32)
    Wo = np.asarray(L0_attn_Wo, dtype=np.float32)
    bo = np.asarray(L0_attn_bo, dtype=np.float32).reshape(1, D)
    W1 = np.asarray(L0_ffn_W1, dtype=np.float32)
    W2 = np.asarray(L0_ffn_W2, dtype=np.float32)
    W3 = np.asarray(L0_ffn_W3, dtype=np.float32)

    WoT = np.ascontiguousarray(Wo.T)
    def tile_up(W_):
        t = W_.T.reshape(16, 128, 64, 128)
        return np.ascontiguousarray(t.transpose(2, 1, 0, 3)).reshape(64, 128, 16 * 128)
    w1tp = tile_up(W1)
    w3tp = tile_up(W3)
    w2ch = np.ascontiguousarray(
        W2.reshape(8, 256, 64, 128).transpose(0, 3, 2, 1))

    H128 = _hadamard(128)
    I2H64 = np.kron(np.eye(2, dtype=np.float32), _hadamard(64)).astype(np.float32)
    ident = np.eye(128, dtype=np.float32)
    qi, ki = np.arange(128)[:, None], np.arange(128)[None, :]
    tri = np.where(ki <= qi, 0.0, NEG).astype(np.float32)
    mask_p = [np.concatenate([np.zeros((128, 128), np.float32), tri], axis=1),
              np.concatenate([tri, np.full((128, 128), NEG, np.float32)], axis=1)]

    nc = _get_nc()
    in_maps = []
    perms = []
    for c in range(8):
        b, P = c // 2, c % 2
        tiles = [31 - 2 * j - P for j in range(NSLOT)]
        perm = np.concatenate([np.arange(t * 128, (t + 1) * 128) for t in tiles])
        perms.append((b, perm))
        xb = x[b]
        xbT = np.ascontiguousarray(xb.T)
        in_maps.append({
            "xqT": np.ascontiguousarray(xbT[:, perm]),
            "xq": np.ascontiguousarray(xb[perm]),
            "xkT": xbT,
            "xk": np.ascontiguousarray(xb),
            "ws": Ws, "wa": Wa, "wot": WoT, "bo": bo,
            "w1tp": w1tp, "w3tp": w3tp, "w2ch": w2ch,
            "h128": H128, "i2h64": I2H64, "ident": ident,
            "mask": mask_p[P],
        })

    res = run_bass_kernel_spmd(nc, in_maps, core_ids=list(range(8)))
    out = np.empty((B, L, D), dtype=np.float32)
    for c in range(8):
        b, perm = perms[c]
        out[b][perm] = res.results[c]["out"]
    return out


# version 34
# speedup vs baseline: 1.1511x; 1.1426x over previous
import numpy as np
from contextlib import ExitStack

import concourse.bass as bass
import concourse.mybir as mybir
from concourse.bass_utils import run_bass_kernel_spmd

F32 = mybir.dt.float32
BF16 = mybir.dt.bfloat16
AF = mybir.ActivationFunctionType
ALU = mybir.AluOpType
AX = mybir.AxisListType

B, L, D, R, HF = 4, 4096, 2048, 256, 8192
TQ = 2048
NSLOT = 16
CJ = [32 - 2 * j for j in range(NSLOT)]
NGRP = 4
GSL = 4
GMAX = [CJ[GSL * g] for g in range(NGRP)]
FT = 1024
NEG = -1.0e30
EPS = 1.0e-5
INV16 = 1.0 / 16.0
FWSCALE = float(1.0 / np.sqrt(HF))

ENGINES = ("tensor", "vector", "scalar", "sync", "gpsimd")


class _Op:
    __slots__ = ("eng", "fn", "deps", "marked", "value", "is_dma", "sem_key",
                 "dma_waits", "snap")

    def __init__(self):
        self.deps = set()
        self.marked = False
        self.value = 0
        self.is_dma = False
        self.sem_key = None
        self.dma_waits = {}


class Emitter:
    def __init__(self):
        self.eng_ops = {e: [] for e in ENGINES}
        self.writer = {}
        self.readers = {}
        self.barrier_ops = []
        self.dma_counts = {}

    def op(self, eng, fn, reads=(), writes=(), dma=None):
        o = _Op()
        o.eng = eng
        o.fn = fn
        if dma is not None:
            o.is_dma = True
            o.sem_key = dma
        deps = o.deps
        for k in reads:
            w = self.writer.get(k)
            if w is not None:
                deps.add(w)
        for k in writes:
            rs = self.readers.get(k)
            if rs:
                deps.update(rs)
            w = self.writer.get(k)
            if w is not None:
                deps.add(w)
        deps.update(self.barrier_ops)
        deps.discard(o)
        if eng == "tensor":
            for p in [p for p in deps if p.eng == "tensor" and not p.is_dma]:
                deps.discard(p)
        for p in list(deps):
            if p.is_dma:
                v = 16 * self.dma_counts[p.sem_key]
                if o.dma_waits.get(p.sem_key, 0) < v:
                    o.dma_waits[p.sem_key] = v
                deps.discard(p)
            else:
                p.marked = True
        if dma is not None:
            self.dma_counts[dma] = self.dma_counts.get(dma, 0) + 1
        for k in reads:
            self.readers.setdefault(k, []).append(o)
        for k in writes:
            self.writer[k] = o
            self.readers[k] = []
        self.eng_ops[eng].append(o)
        return o

    def barrier(self):
        self.barrier_ops = [lst[-1] for lst in self.eng_ops.values() if lst]
        for p in self.barrier_ops:
            if p.is_dma:
                pass
            else:
                p.marked = True

    def finalize(self):
        for e in ("tensor", "vector", "scalar"):
            v = 0
            for o in self.eng_ops[e]:
                if o.marked:
                    v += 1
                    o.value = v

    def replay(self, eng_name, handle, eng_sems, dma_sems):
        waited = {}
        for o in self.eng_ops[eng_name]:
            waits = {}
            for p in o.deps:
                if p.is_dma:
                    continue
                key = ("eng", p.eng)
                if waits.get(key, 0) < p.value:
                    waits[key] = p.value
            for sk, v in o.dma_waits.items():
                key = ("dma", sk)
                if waits.get(key, 0) < v:
                    waits[key] = v
            for key, v in waits.items():
                if waited.get(key, 0) >= v:
                    continue
                waited[key] = v
                sem = eng_sems[key[1]] if key[0] == "eng" else dma_sems[key[1]]
                handle.wait_ge(sem, v)
            ins = o.fn(handle)
            if o.is_dma:
                ins.then_inc(dma_sems[o.sem_key], 16)
            elif o.marked:
                ins.then_inc(eng_sems[o.eng], 1)


def _hadamard(n):
    h = np.array([[1.0]], dtype=np.float32)
    while h.shape[0] < n:
        h = np.block([[h, h], [h, -h]]).astype(np.float32)
    return h


def build_nc():
    nc = bass.Bass("TRN2", target_bir_lowering=False)
    E = Emitter()

    d_xqT = nc.dram_tensor("xqT", [D, TQ], F32, kind="ExternalInput").ap()
    d_xq = nc.dram_tensor("xq", [TQ, D], F32, kind="ExternalInput").ap()
    d_xkT = nc.dram_tensor("xkT", [D, L], F32, kind="ExternalInput").ap()
    d_xk = nc.dram_tensor("xk", [L, D], F32, kind="ExternalInput").ap()
    d_ws = nc.dram_tensor("ws", [D, R], F32, kind="ExternalInput").ap()
    d_wa = nc.dram_tensor("wa", [D, R], F32, kind="ExternalInput").ap()
    d_wot = nc.dram_tensor("wot", [D, D], F32, kind="ExternalInput").ap()
    d_bo = nc.dram_tensor("bo", [1, D], F32, kind="ExternalInput").ap()
    d_w1 = nc.dram_tensor("w1tp", [64, 128, 16 * 128], F32, kind="ExternalInput").ap()
    d_w3 = nc.dram_tensor("w3tp", [64, 128, 16 * 128], F32, kind="ExternalInput").ap()
    d_w2 = nc.dram_tensor("w2ch", [8, 128, 64, 256], F32, kind="ExternalInput").ap()
    d_h128 = nc.dram_tensor("h128", [128, 128], F32, kind="ExternalInput").ap()
    d_i2h64 = nc.dram_tensor("i2h64", [128, 128], F32, kind="ExternalInput").ap()
    d_ident = nc.dram_tensor("ident", [128, 128], F32, kind="ExternalInput").ap()
    d_mask = nc.dram_tensor("mask", [128, 256], F32, kind="ExternalInput").ap()
    d_out = nc.dram_tensor("out", [TQ, D], F32, kind="ExternalOutput").ap()
    d_y1 = nc.dram_tensor("y1d", [TQ, D], F32, kind="Internal").ap()
    d_zz = nc.dram_tensor("zz", [TQ, D], F32, kind="Internal").ap()
    d_vb = nc.dram_tensor("vb", [L, D], BF16, kind="Internal").ap()
    d_hid = nc.dram_tensor("hid", [64, 128, TQ], BF16, kind="Internal").ap()
    d_tmp = nc.dram_tensor("w2tmp", [8, 128, 64, 256], BF16, kind="Internal").ap()
    d_w2p = nc.dram_tensor("w2p", [64, 128, D], BF16, kind="Internal").ap()

    with ExitStack() as S0:
        C = S0.enter_context
        ident_bf = C(nc.sbuf_tensor("ident_bf", [128, 128], BF16))
        h128_bf = C(nc.sbuf_tensor("h128_bf", [128, 128], BF16))
        i2h64_bf = C(nc.sbuf_tensor("i2h64_bf", [128, 128], BF16))
        mask_sb = C(nc.sbuf_tensor("mask_sb", [128, 256], F32))
        bo_bf = C(nc.sbuf_tensor("bo_bf", [1, D], BF16))
        ones1 = C(nc.sbuf_tensor("ones1", [1, 128], BF16))
        lns = C(nc.sbuf_tensor("lns", [128, 32], F32))
        nmax = C(nc.sbuf_tensor("nmax", [128, 1], F32))
        sums = C(nc.sbuf_tensor("sums", [128, 1], F32))
        inv1 = C(nc.sbuf_tensor("inv1", [128, 1], F32))

        E.op("gpsimd", lambda g: g.dma_start(out=ident_bf[:], in_=d_ident),
             writes=("ident_bf",), dma="ident")
        E.op("gpsimd", lambda g: g.dma_start(out=h128_bf[:], in_=d_h128),
             writes=("h128_bf",), dma="h128")
        E.op("gpsimd", lambda g: g.dma_start(out=i2h64_bf[:], in_=d_i2h64),
             writes=("i2h64_bf",), dma="i2h64")
        E.op("gpsimd", lambda g: g.dma_start(out=bo_bf[:], in_=d_bo),
             writes=("bo_bf",), dma="bold")
        E.op("sync", lambda s: s.dma_start(out=mask_sb[:], in_=d_mask),
             writes=("mask_sb",), dma="maskld")
        E.op("vector", lambda v: v.memset(ones1[:], 1.0), writes=("ones1",))

        DC = 256
        NCH = D // DC
        d_tmp_s = d_tmp.rearrange("ch (s l) hi d -> ch s hi l d", s=2)
        d_w2p_s = d_w2p.rearrange("c (s l) t -> s c l t", s=2)
        with ExitStack() as ST:
            CT = ST.enter_context
            rhs1 = [CT(nc.sbuf_tensor(f"rhs1_{i}", [128, 64, DC], BF16)) for i in range(2)]
            t2sb = [CT(nc.sbuf_tensor(f"t2sb_{i}", [128, 64, DC], BF16)) for i in range(1)]
            t1full = [CT(nc.sbuf_tensor(f"t1full_{i}", [128, 64, DC], BF16)) for i in range(1)]
            t2full = [CT(nc.sbuf_tensor(f"t2full_{i}", [128, 64, DC], BF16)) for i in range(1)]
            pt1 = [CT(nc.psum_tensor(f"pt1_{i}", [128, 512], F32)) for i in range(4)]
            pt2 = [CT(nc.psum_tensor(f"pt2_{i}", [128, 512], F32)) for i in range(4)]
            NSC = (64 * DC) // 512

            def emit_load(ch):
                ri = ch % len(rhs1)
                buf = rhs1[ri]
                E.op("gpsimd",
                     (lambda g, ch=ch, buf=buf: g.dma_start(
                         out=buf[:], in_=d_w2[ch])),
                     writes=(f"rhs1:{ri}",), dma=f"w2in{ch % 2}")

            def emit_vb(ch):
                E.op("gpsimd",
                     (lambda g, ch=ch: g.dma_start(
                         out=d_vb[ch * 128:(ch + 1) * 128, :],
                         in_=d_xk[ch * 128:(ch + 1) * 128, :])),
                     writes=(f"vb:{ch}",), dma=f"vbs{ch // 8}")

            def emit_T1(ch):
                ri = ch % len(rhs1)
                buf = rhs1[ri]
                rflat = buf.rearrange("p a b -> p (a b)")
                t1b = t1full[0]
                t1flat = t1b.rearrange("p a b -> p (a b)")
                for c in range(NSC):
                    ps = pt1[c % 4]
                    E.op("tensor",
                         (lambda t, c=c, ps=ps, rflat=rflat:
                          t.matmul(ps[:], h128_bf[:],
                                   rflat[:, c * 512:(c + 1) * 512],
                                   start=True, stop=True,
                                   skip_group_check=True)),
                         reads=(f"rhs1:{ri}", "h128_bf"),
                         writes=(f"pt1{c % 4}",))
                    if c % 2 == 0:
                        E.op("vector",
                             (lambda h, c=c, ps=ps, t1flat=t1flat: h.tensor_scalar(
                                 t1flat[:, c * 512:(c + 1) * 512],
                                 ps[:], FWSCALE, None, ALU.mult)),
                             reads=(f"pt1{c % 4}",), writes=(f"t1f:0:{c}",))
                    else:
                        E.op("scalar",
                             (lambda h, c=c, ps=ps, t1flat=t1flat: h.activation(
                                 t1flat[:, c * 512:(c + 1) * 512],
                                 ps[:], AF.Copy, scale=FWSCALE)),
                             reads=(f"pt1{c % 4}",), writes=(f"t1f:0:{c}",))
                E.op("sync",
                     (lambda s, ch=ch, t1b=t1b: s.dma_start(
                         out=d_tmp[ch], in_=t1b[:])),
                     reads=tuple(f"t1f:0:{c}" for c in range(NSC)),
                     writes=(f"tmpD:{ch}",), dma=f"tmpw{ch % 2}")

            def emit_T2(ch):
                d0 = ch * DC
                buf = t2sb[0]
                rflat = buf.rearrange("p a b -> p (a b)")
                t2b = t2full[0]
                t2flat = t2b.rearrange("p a b -> p (a b)")
                for sh in range(2):
                    E.op("sync",
                         (lambda s, ch=ch, buf=buf, sh=sh: s.dma_start(
                             out=buf[sh * 64:(sh + 1) * 64, :, :],
                             in_=d_tmp_s[ch, sh])),
                         reads=(f"tmpD:{ch}",),
                         writes=("t2sb:0",), dma=f"tmpr{ch % 2}")
                for c in range(NSC):
                    ps = pt2[c % 4]
                    E.op("tensor",
                         (lambda t, c=c, ps=ps, rflat=rflat:
                          t.matmul(ps[:], i2h64_bf[:],
                                   rflat[:, c * 512:(c + 1) * 512],
                                   start=True, stop=True,
                                   skip_group_check=True)),
                         reads=("t2sb:0", "i2h64_bf"),
                         writes=(f"pt2{c % 4}",))
                    if c % 2 == 0:
                        E.op("vector",
                             (lambda h, c=c, ps=ps, t2flat=t2flat: h.tensor_copy(
                                 t2flat[:, c * 512:(c + 1) * 512], ps[:])),
                             reads=(f"pt2{c % 4}",), writes=(f"t2f:0:{c}",))
                    else:
                        E.op("scalar",
                             (lambda h, c=c, ps=ps, t2flat=t2flat: h.activation(
                                 t2flat[:, c * 512:(c + 1) * 512], ps[:], AF.Copy)),
                             reads=(f"pt2{c % 4}",), writes=(f"t2f:0:{c}",))
                for sh in range(2):
                    E.op("sync",
                         (lambda s, ch=ch, d0=d0, sh=sh, t2b=t2b: s.dma_start(
                             out=d_w2p_s[sh, :, :, d0:d0 + DC],
                             in_=t2b[sh * 64:(sh + 1) * 64, :, :])),
                         reads=tuple(f"t2f:0:{c}" for c in range(NSC)),
                         writes=(f"w2p:{ch}",), dma="w2pw")

            emit_load(0)
            emit_load(1)
            for ch in range(NCH):
                emit_T1(ch)
                if ch + 2 < NCH:
                    emit_load(ch + 2)
                for q in range(4):
                    emit_vb(4 * ch + q)
                if ch >= 2:
                    emit_T2(ch - 2)
            emit_T2(NCH - 2)
            emit_T2(NCH - 1)
            E.barrier()

        SA = ExitStack()
        CA = SA.enter_context
        st_bf = [CA(nc.sbuf_tensor(f"st{r}", [128, TQ], BF16)) for r in range(2)]
        ac_bf = [CA(nc.sbuf_tensor(f"ac{r}", [128, L], BF16)) for r in range(2)]
        with ExitStack() as S1:
            C1 = S1.enter_context
            ws_bf = C1(nc.sbuf_tensor("ws_bf", [128, 16 * R], BF16))
            wa_bf = C1(nc.sbuf_tensor("wa_bf", [128, 16 * R], BF16))
            xsb = [C1(nc.sbuf_tensor(f"xsb{i}", [128, 2048], BF16)) for i in range(4)]
            pj = [C1(nc.psum_tensor(f"pj{i}", [128, 512], F32)) for i in range(8)]

            for k in range(16):
                E.op("gpsimd",
                     (lambda g, k=k: g.dma_start(
                         out=ws_bf[:, k * R:(k + 1) * R],
                         in_=d_ws[k * 128:(k + 1) * 128, :])),
                     writes=(f"ws:{k}",), dma="wsld")
                E.op("gpsimd",
                     (lambda g, k=k: g.dma_start(
                         out=wa_bf[:, k * R:(k + 1) * R],
                         in_=d_wa[k * 128:(k + 1) * 128, :])),
                     writes=(f"wa:{k}",), dma="wald")

            def proj(which, wbuf, wkey, src, src_col0, dst, dst_col0):
                for k in range(16):
                    buf = xsb[k % 4]
                    E.op("gpsimd",
                         (lambda g, k=k, buf=buf: g.dma_start(
                             out=buf[:],
                             in_=src[k * 128:(k + 1) * 128,
                                     src_col0:src_col0 + 2048])),
                         reads=(), writes=(f"xsb{k % 4}",), dma=f"xsb{k % 4}")
                    for m in range(2):
                        for n in range(4):
                            p = pj[m * 4 + n]
                            E.op("tensor",
                                 (lambda t, k=k, m=m, n=n, p=p, buf=buf:
                                  t.matmul(p[:],
                                           wbuf[:, k * R + m * 128:k * R + (m + 1) * 128],
                                           buf[:, n * 512:(n + 1) * 512],
                                           start=(k == 0), stop=(k == 15),
                                           skip_group_check=True)),
                                 reads=(f"{wkey}:{k}", f"xsb{k % 4}"),
                                 writes=(f"pj{m * 4 + n}",))
                for m in range(2):
                    for n in range(4):
                        p = pj[m * 4 + n]
                        scale = INV16 if which == "state" else None
                        if scale is not None:
                            E.op("vector",
                                 (lambda v, m=m, n=n, p=p:
                                  v.tensor_scalar(
                                      dst[m][:, dst_col0 + n * 512:dst_col0 + (n + 1) * 512],
                                      p[:], INV16, None, ALU.mult)),
                                 reads=(f"pj{m * 4 + n}",),
                                 writes=(f"{'st' if which == 'state' else 'ac'}:{m}:{dst_col0 + n * 512}",))
                        else:
                            E.op("vector",
                                 (lambda v, m=m, n=n, p=p:
                                  v.tensor_copy(
                                      dst[m][:, dst_col0 + n * 512:dst_col0 + (n + 1) * 512],
                                      p[:])),
                                 reads=(f"pj{m * 4 + n}",),
                                 writes=(f"ac:{m}:{dst_col0 + n * 512}",))

            proj("state", ws_bf, "ws", d_xqT, 0, st_bf, 0)
            proj("action", wa_bf, "wa", d_xkT, 0, ac_bf, 0)
            proj("action", wa_bf, "wa", d_xkT, 2048, ac_bf, 2048)
            E.barrier()

        with ExitStack() as S2:
            C2 = S2.enter_context
            wot_bf = C2(nc.sbuf_tensor("wot_bf", [128, 16 * D], BF16))
            Praw = C2(nc.sbuf_tensor("Praw", [128, 4096], F32))
            Pbf2 = [C2(nc.sbuf_tensor(f"Pbf{i}", [128, 4096], BF16)) for i in range(2)]
            PT2 = [C2(nc.sbuf_tensor(f"PT{i}", [128, 32 * 256], BF16)) for i in range(2)]
            atT = C2(nc.sbuf_tensor("atT", [128, 16 * 256], BF16))
            v_c = [C2(nc.sbuf_tensor(f"vc{i}", [128, 1024], BF16)) for i in range(4)]
            z1b = [C2(nc.sbuf_tensor(f"z1b{i}", [128, D], F32)) for i in range(2)]
            y1b = C2(nc.sbuf_tensor("y1b", [128, D], F32))
            xqt = [C2(nc.sbuf_tensor("xqt0", [128, D], F32))] * 2
            ptps = [C2(nc.psum_tensor(f"ptps{i}", [128, 128], BF16)) for i in range(2)]
            sc = [C2(nc.psum_tensor(f"sc{i}", [128, 512], F32)) for i in range(2)]
            av = [C2(nc.psum_tensor(f"av{i}", [128, 512], F32)) for i in range(4)]

            for k in range(16):
                E.op("gpsimd",
                     (lambda g, k=k: g.dma_start(
                         out=wot_bf[:, k * D:(k + 1) * D],
                         in_=d_wot[k * 128:(k + 1) * 128, :])),
                     writes=(f"wot:{k}",), dma="wotld")

            def emit_scores(p, i):
                j = 2 * p + i
                cj = CJ[j]
                W = cj * 128
                Pbf = Pbf2[i]
                pbk = f"pbf{i}"
                off = 0
                bi = 0
                while off < W:
                    wdt = min(512, W - off)
                    ps = sc[bi % 2]
                    for r in range(2):
                        E.op("tensor",
                             (lambda t, r=r, j=j, off=off, wdt=wdt, ps=ps:
                              t.matmul(ps[:, 0:wdt],
                                       st_bf[r][:, j * 128:(j + 1) * 128],
                                       ac_bf[r][:, off:off + wdt],
                                       start=(r == 0), stop=(r == 1))),
                             reads=(f"st:{r}:{(j * 128) // 512 * 512}",
                                    f"ac:{r}:{(off // 512) * 512}",
                                    f"ac:{r}:{((off + wdt - 1) // 512) * 512}"),
                             writes=(f"scp{bi % 2}",))
                    if off + wdt == W:
                        if wdt == 512:
                            E.op("vector",
                                 (lambda v, off=off, ps=ps:
                                  v.tensor_copy(Praw[:, off:off + 256], ps[:, 0:256])),
                                 reads=(f"scp{bi % 2}",), writes=("praw",))
                            E.op("vector",
                                 (lambda v, off=off, ps=ps:
                                  v.tensor_tensor(Praw[:, off + 256:off + 512],
                                                  ps[:, 256:512], mask_sb[:], ALU.add)),
                                 reads=(f"scp{bi % 2}", "mask_sb"), writes=("praw",))
                        else:
                            E.op("vector",
                                 (lambda v, off=off, ps=ps:
                                  v.tensor_tensor(Praw[:, off:off + 256],
                                                  ps[:, 0:256], mask_sb[:], ALU.add)),
                                 reads=(f"scp{bi % 2}", "mask_sb"), writes=("praw",))
                    else:
                        E.op("vector",
                             (lambda v, off=off, wdt=wdt, ps=ps:
                              v.tensor_copy(Praw[:, off:off + wdt], ps[:, 0:wdt])),
                             reads=(f"scp{bi % 2}",), writes=("praw",))
                    off += wdt
                    bi += 1
                E.op("vector",
                     (lambda v, W=W: v.reduce_max(nmax[:], Praw[:, 0:W],
                                                  axis=AX.X, negate=True)),
                     reads=("praw",), writes=("nmax",))
                E.op("scalar",
                     (lambda s, W=W, Pbf=Pbf: s.activation(
                         Pbf[:, 0:W], Praw[:, 0:W], AF.Exp, bias=nmax[:],
                         accum_out=sums[:])),
                     reads=("praw", "nmax"), writes=(pbk, "sums"))
                E.op("vector", lambda v: v.reciprocal(inv1[:], sums[:]),
                     reads=("sums",), writes=("inv1",))
                E.op("vector",
                     (lambda v, W=W, Pbf=Pbf: v.tensor_scalar_mul(
                         Pbf[:, 0:W], Pbf[:, 0:W], inv1[:])),
                     reads=(pbk, "inv1"), writes=(pbk,))

            def emit_av_memsets(p, half):
                for b in range(4):
                    E.op("vector",
                         (lambda v, b=b: v.memset(av[b][:], 0.0)),
                         writes=(f"av{b}",))

            def emit_av_half(p, half, inter=None):
                maxp = CJ[2 * p]
                for ch in range(maxp):
                    buf = v_c[ch % 4]
                    E.op("sync",
                         (lambda g2, ch=ch, half=half, buf=buf: g2.dma_start(
                             out=buf[:],
                             in_=d_vb[ch * 128:(ch + 1) * 128,
                                      half * 1024:(half + 1) * 1024])),
                         reads=(f"vb:{ch}",),
                         writes=(f"vc{ch % 4}",), dma=f"vc{ch % 4}")
                    for m in range(8):
                        E.op("tensor",
                             (lambda t, ch=ch, m=m, maxp=maxp, buf=buf, p=p:
                              t.matmul(av[m // 2][:, (m % 2) * 256:(m % 2) * 256 + 256],
                                       buf[:, m * 128:(m + 1) * 128],
                                       PT2[p % 2][:, ch * 256:(ch + 1) * 256],
                                       start=False, stop=(ch == maxp - 1),
                                       skip_group_check=True)),
                             reads=(f"vc{ch % 4}", f"pt:{ch}:0", f"pt:{ch}:1"),
                             writes=(f"av{m // 2}",))
                    if inter:
                        n_per = max(1, len(inter) // maxp + 1)
                        for _ in range(n_per):
                            if inter:
                                inter.pop(0)()
                for m in range(8):
                    hc = half * 8 + m
                    E.op("vector",
                         (lambda v, hc=hc, m=m:
                          v.tensor_copy(atT[:, hc * 256:(hc + 1) * 256],
                                        av[m // 2][:, (m % 2) * 256:(m % 2) * 256 + 256])),
                         reads=(f"av{m // 2}",), writes=(f"atT:{hc}",))
                if inter:
                    while inter:
                        inter.pop(0)()

            def pt_thunks(p):
                maxp = CJ[2 * p]
                thunks = []
                for i in range(2):
                    cj = CJ[2 * p + i]
                    Pbf = Pbf2[i]
                    pbk = f"pbf{i}"
                    for kb in range(cj):
                        pp = ptps[(len(thunks)) % 2]
                        def th(kb=kb, i=i, pp=pp, Pbf=Pbf, pbk=pbk, p=p,
                               ppk=f"ptps{len(thunks) % 2}"):
                            E.op("tensor",
                                 (lambda t: t.transpose(
                                     pp[:], Pbf[:, kb * 128:(kb + 1) * 128],
                                     ident_bf[:])),
                                 reads=(pbk, "ident_bf"), writes=(ppk,))
                            E.op("vector",
                                 (lambda v: v.tensor_copy(
                                     PT2[p % 2][:, kb * 256 + i * 128:kb * 256 + (i + 1) * 128],
                                     pp[:])),
                                 reads=(ppk,), writes=(f"pt:{kb}:{i}",))
                        thunks.append(th)
                    for kb in range(cj, maxp):
                        def thz(kb=kb, i=i, p=p):
                            E.op("vector",
                                 (lambda v: v.memset(
                                     PT2[p % 2][:, kb * 256 + i * 128:kb * 256 + (i + 1) * 128], 0.0)),
                                 writes=(f"pt:{kb}:{i}",))
                        thunks.append(thz)
                return thunks

            def emit_wo_epi(p):
                for i in range(2):
                    j = 2 * p + i
                    xq_b = xqt[i]
                    E.op("sync",
                         (lambda s, j=j, xq_b=xq_b: s.dma_start(
                             out=xq_b[:], in_=d_xq[j * 128:(j + 1) * 128, :])),
                         writes=("xqt0",), dma="xqtld")
                    z1 = z1b[i]
                    zkey = f"z1b{i}"
                    for n in range(4):
                        ps = sc[n % 2]
                        for k in range(16):
                            E.op("tensor",
                                 (lambda t, k=k, n=n, i=i, ps=ps:
                                  t.matmul(ps[:],
                                           atT[:, k * 256 + i * 128:k * 256 + (i + 1) * 128],
                                           wot_bf[:, k * D + n * 512:k * D + (n + 1) * 512],
                                           start=(k == 0), stop=False,
                                           skip_group_check=True)),
                                 reads=(f"atT:{k}", f"wot:{k}"),
                                 writes=(f"scp{n % 2}",))
                        E.op("tensor",
                             (lambda t, n=n, ps=ps:
                              t.matmul(ps[:], ones1[:],
                                       bo_bf[:, n * 512:(n + 1) * 512],
                                       start=False, stop=True,
                                       skip_group_check=True)),
                             reads=("ones1", "bo_bf"), writes=(f"scp{n % 2}",))
                        E.op("vector",
                             (lambda v, n=n, ps=ps, z1=z1, xq_b=xq_b:
                              v.tensor_tensor(z1[:, n * 512:(n + 1) * 512], ps[:],
                                              xq_b[:, n * 512:(n + 1) * 512], ALU.add)),
                             reads=(f"scp{n % 2}", "xqt0"), writes=(zkey,))
                _ln_pair(E, z1b, ("z1b0", "z1b1"), y1b, lns, Praw, d_y1, p)

            order = [7, 6, 5, 4, 3, 2, 1, 0]
            for step in range(9):
                prev = order[step - 1] if step >= 1 else None
                cur = order[step] if step < 8 else None
                if prev is not None:
                    emit_av_memsets(prev, 0)
                if cur is not None:
                    emit_scores(cur, 0)
                    emit_scores(cur, 1)
                if prev is not None:
                    emit_av_half(prev, 0)
                    emit_av_memsets(prev, 1)
                    emit_av_half(prev, 1,
                                 inter=(pt_thunks(cur) if cur is not None else None))
                    emit_wo_epi(prev)
                elif cur is not None:
                    for th in pt_thunks(cur):
                        th()
            E.barrier()
        SA.close()

        with ExitStack() as SB1:
            CB = SB1.enter_context
            y1t = CB(nc.sbuf_tensor("y1t", [128, 16 * TQ], BF16))
            w1b = [CB(nc.sbuf_tensor(f"w1b{i}", [128, 16 * 128], BF16)) for i in range(3)]
            w3b = [CB(nc.sbuf_tensor(f"w3b{i}", [128, 16 * 128], BF16)) for i in range(3)]
            y1bf = [CB(nc.sbuf_tensor(f"y1bf{i}", [128, D], BF16)) for i in range(4)]
            hstg = [CB(nc.sbuf_tensor(f"hstg{i}", [128, 512], BF16)) for i in range(4)]
            silu = [CB(nc.sbuf_tensor(f"silu{i}", [128, 512], BF16)) for i in range(2)]
            trps = CB(nc.psum_tensor("trps", [128, 128], BF16))
            up = [CB(nc.psum_tensor(f"up{i}", [128, 512], F32)) for i in range(4)]

            for m in range(16):
                buf = y1bf[m % 4]
                E.op("gpsimd",
                     (lambda g2, m=m, buf=buf: g2.dma_start(
                         out=buf[:],
                         in_=d_y1[m * 128:(m + 1) * 128, :])),
                     reads=(f"y1d:{m}",),
                     writes=(f"y1bf{m % 4}",), dma=f"y1bf{m % 4}")
                for c in range(16):
                    E.op("tensor",
                         (lambda t, c=c, buf=buf, m=m:
                          t.transpose(trps[:], buf[:, c * 128:(c + 1) * 128],
                                      ident_bf[:])),
                         reads=(f"y1bf{m % 4}", "ident_bf"), writes=("trps",))
                    E.op("vector",
                         (lambda v, c=c, m=m:
                          v.tensor_copy(
                              y1t[:, c * TQ + m * 128:c * TQ + (m + 1) * 128],
                              trps[:])),
                         reads=("trps",), writes=(f"y1t:{c}:{m}",))
            for hc in range(64):
                bw1, bw3 = w1b[hc % 3], w3b[hc % 3]
                E.op("gpsimd",
                     (lambda g2, hc=hc, bw1=bw1: g2.dma_start(
                         out=bw1[:], in_=d_w1[hc])),
                     writes=(f"w1b{hc % 3}",), dma=f"w1b{hc % 3}")
                E.op("gpsimd",
                     (lambda g2, hc=hc, bw3=bw3: g2.dma_start(
                         out=bw3[:], in_=d_w3[hc])),
                     writes=(f"w3b{hc % 3}",), dma=f"w3b{hc % 3}")
                for n in range(4):
                    p1 = up[n % 2]
                    p3 = up[2 + n % 2]
                    for k in range(16):
                        E.op("tensor",
                             (lambda t, k=k, n=n, p1=p1, bw1=bw1:
                              t.matmul(p1[:],
                                       bw1[:, k * 128:(k + 1) * 128],
                                       y1t[:, k * TQ + n * 512:k * TQ + n * 512 + 512],
                                       start=(k == 0), stop=(k == 15),
                                       skip_group_check=True)),
                             reads=(f"w1b{hc % 3}",) + tuple(
                                 f"y1t:{k}:{n * 4 + mm}" for mm in range(4)),
                             writes=(f"up{n % 2}",))
                    for k in range(16):
                        E.op("tensor",
                             (lambda t, k=k, n=n, p3=p3, bw3=bw3:
                              t.matmul(p3[:],
                                       bw3[:, k * 128:(k + 1) * 128],
                                       y1t[:, k * TQ + n * 512:k * TQ + n * 512 + 512],
                                       start=(k == 0), stop=(k == 15),
                                       skip_group_check=True)),
                             reads=(f"w3b{hc % 3}",) + tuple(
                                 f"y1t:{k}:{n * 4 + mm}" for mm in range(4)),
                             writes=(f"up{2 + n % 2}",))
                    sl = silu[n % 2]
                    hb = hstg[n]
                    E.op("scalar",
                         (lambda s, p1=p1, sl=sl: s.activation(sl[:], p1[:], AF.Silu)),
                         reads=(f"up{n % 2}",), writes=(f"silu{n % 2}",))
                    E.op("vector",
                         (lambda v, p3=p3, sl=sl, hb=hb:
                          v.tensor_tensor(hb[:], sl[:], p3[:], ALU.mult)),
                         reads=(f"silu{n % 2}", f"up{2 + n % 2}"),
                         writes=(f"hstg{n}",))
                    E.op("sync",
                         (lambda s2, hc=hc, n=n, hb=hb: s2.dma_start(
                             out=d_hid[hc][:, n * 512:(n + 1) * 512], in_=hb[:])),
                         reads=(f"hstg{n}",), writes=(f"hid:{hc}:{n}",), dma="hidw")
            E.barrier()

        FT2 = 512
        with ExitStack() as SB2:
            CB2 = SB2.enter_context
            Hd2r = [CB2(nc.sbuf_tensor(f"Hd2_{i}", [128, 64 * FT2], BF16)) for i in range(2)]
            zt = CB2(nc.sbuf_tensor("zt", [128, D], F32))
            yt = CB2(nc.sbuf_tensor("yt", [128, D], F32))
            ot = CB2(nc.sbuf_tensor("ot", [128, D], F32))
            sqc = CB2(nc.sbuf_tensor("sqc", [128, D], BF16))
            w2h = [CB2(nc.sbuf_tensor(f"w2h{i}", [128, 8 * 512], BF16)) for i in range(4)]
            z2s = [CB2(nc.sbuf_tensor(f"z2s{i}", [128, 512], F32)) for i in range(4)]
            dps = [CB2(nc.psum_tensor(f"dps{i}", [128, 512], F32)) for i in range(8)]

            def emit_C(tc):
                for mc in range(tc * 4, tc * 4 + 4):
                    E.op("sync",
                         (lambda s, mc=mc: s.dma_start(
                             out=zt[:], in_=d_zz[mc * 128:(mc + 1) * 128, :])),
                         reads=tuple(f"zz:{mc // 4}:{mc % 4}:{n}" for n in range(4)),
                         writes=("zt",), dma="ztld")
                    E.op("gpsimd",
                         (lambda s, mc=mc: s.dma_start(
                             out=yt[:], in_=d_y1[mc * 128:(mc + 1) * 128, :])),
                         reads=(f"y1d:{mc}",), writes=("yt",), dma="ytld")
                    E.op("vector",
                         (lambda v: v.tensor_add(zt[:], zt[:], yt[:])),
                         reads=("zt", "yt"), writes=("zt",))
                    _ln_norm(E, nc, zt, "zt", ot, "ot", lns, sqc, "sqc")
                    E.op("sync",
                         (lambda s, mc=mc: s.dma_start(
                             out=d_out[mc * 128:(mc + 1) * 128, :], in_=ot[:])),
                         reads=("ot",), writes=(f"out:{mc}",), dma="outw")

            d_hid_r = d_hid.rearrange("c p t -> p c t")
            d_w2p_r = d_w2p.rearrange("c p t -> p c t")
            def emit_h2l(tt):
                Hd2 = Hd2r[tt % 2]
                for hc0 in range(0, 64, 8):
                    rd = tuple(f"hid:{hc}:{tt}" for hc in range(hc0, hc0 + 8))
                    E.op("gpsimd",
                         (lambda s2, hc0=hc0, tt=tt, Hd2=Hd2: s2.dma_start(
                             out=Hd2[:, hc0 * FT2:(hc0 + 8) * FT2],
                             in_=d_hid_r[:, hc0:hc0 + 8,
                                         tt * FT2:(tt + 1) * FT2])),
                         reads=rd, writes=(f"Hd2:{tt % 2}:{hc0}",), dma="h2l")

            emit_h2l(0)
            for tt in range(4):
                if tt + 1 < 4:
                    emit_h2l(tt + 1)
                if tt >= 1:
                    emit_C(tt - 1)
                Hd2 = Hd2r[tt % 2]
                hkey = f"Hd2:{tt % 2}"
                dbase = 4 * (tt % 2)
                for n in range(4):
                    for qb in range(8):
                        s_idx = n * 8 + qb
                        buf = w2h[s_idx % 4]
                        hh0 = qb * 8
                        E.op("gpsimd",
                             (lambda g2, hh0=hh0, n=n, buf=buf: g2.dma_start(
                                 out=buf[:],
                                 in_=d_w2p_r[:, hh0:hh0 + 8,
                                             n * 512:(n + 1) * 512])),
                             reads=tuple(f"w2p:{ch}"
                                         for ch in range(2 * n, 2 * n + 2)),
                             writes=(f"w2q{s_idx % 4}",),
                             dma=f"w2h{s_idx % 4}")
                        for m in range(4):
                            for h2 in range(8):
                                hh = hh0 + h2
                                E.op("tensor",
                                     (lambda t, m=m, h2=h2, hh=hh, qb=qb,
                                      buf=buf, Hd2=Hd2, dbase=dbase:
                                      t.matmul(dps[dbase + m][:],
                                               Hd2[:, hh * FT2 + m * 128:hh * FT2 + (m + 1) * 128],
                                               buf[:, h2 * 512:(h2 + 1) * 512],
                                               start=(qb == 0 and h2 == 0),
                                               stop=(qb == 7 and h2 == 7),
                                               skip_group_check=True)),
                                     reads=(f"w2q{s_idx % 4}", f"{hkey}:{hh0}"),
                                     writes=(f"dps{dbase + m}",))
                    for m in range(4):
                        zb = z2s[m]
                        E.op("vector",
                             (lambda v, m=m, zb=zb, dbase=dbase:
                              v.tensor_copy(zb[:], dps[dbase + m][:])),
                             reads=(f"dps{dbase + m}",), writes=(f"z2s{m}",))
                        E.op("sync",
                             (lambda s, tt=tt, m=m, n=n, zb=zb: s.dma_start(
                                 out=d_zz[tt * FT2 + m * 128:tt * FT2 + (m + 1) * 128,
                                          n * 512:(n + 1) * 512],
                                 in_=zb[:])),
                             reads=(f"z2s{m}",), writes=(f"zz:{tt}:{m}:{n}",),
                             dma="zzw")
            emit_C(3)

        E.finalize()

        eng_sems = {}
        dma_sems = {}
        for e in ("tensor", "vector", "scalar"):
            eng_sems[e] = C(nc.semaphore(f"sem_{e}"))
        for sk in E.dma_counts:
            dma_sems[sk] = C(nc.semaphore(f"ds_{sk}"))

        block = C(nc.Block())
        n_out = E.dma_counts["outw"]

        @block.tensor
        def _(t):
            E.replay("tensor", t, eng_sems, dma_sems)

        @block.vector
        def _(v):
            E.replay("vector", v, eng_sems, dma_sems)

        @block.scalar
        def _(s):
            E.replay("scalar", s, eng_sems, dma_sems)

        @block.gpsimd
        def _(g):
            E.replay("gpsimd", g, eng_sems, dma_sems)

        @block.sync
        def _(s):
            E.replay("sync", s, eng_sems, dma_sems)
            s.wait_ge(dma_sems["outw"], 16 * n_out)

    return nc


def _ln_pair(E, z1b, zkeys, y1b, lns, praw, d_y1, p):
    S1, SSQ, MEAN, MSQ, M2, VAR, VPE, SD, R0, RR, H, H2, RSTD, MR, NMR = range(15)

    def col(c, i):
        return lns[:, c + 16 * i:c + 16 * i + 1]

    def vop(i, fn, rd, wr):
        E.op("vector", fn, reads=rd, writes=wr)

    for i in range(2):
        z, zk = z1b[i], zkeys[i]
        E.op("vector", (lambda v, z=z, i=i: v.reduce_sum(col(S1, i), z[:], axis=AX.X)),
             reads=(zk,), writes=(f"ln{i}:s1",))
    for i in range(2):
        z, zk = z1b[i], zkeys[i]
        E.op("scalar",
             (lambda s, z=z, i=i: s.activation(praw[:, 2048:2048 + D], z[:], AF.Square,
                                               accum_out=col(SSQ, i))),
             reads=(zk,), writes=("praw", f"ln{i}:ssq"))
    for i in range(2):
        E.op("vector", (lambda v, i=i: v.tensor_scalar_mul(col(MEAN, i), col(S1, i), 1.0 / D)),
             reads=(f"ln{i}:s1",), writes=(f"ln{i}:mean",))
        E.op("vector", (lambda v, i=i: v.tensor_scalar_mul(col(MSQ, i), col(SSQ, i), 1.0 / D)),
             reads=(f"ln{i}:ssq",), writes=(f"ln{i}:msq",))
        E.op("vector", (lambda v, i=i: v.tensor_mul(col(M2, i), col(MEAN, i), col(MEAN, i))),
             reads=(f"ln{i}:mean",), writes=(f"ln{i}:m2",))
        E.op("vector", (lambda v, i=i: v.tensor_sub(col(VAR, i), col(MSQ, i), col(M2, i))),
             reads=(f"ln{i}:msq", f"ln{i}:m2"), writes=(f"ln{i}:var",))
        E.op("vector", (lambda v, i=i: v.tensor_scalar_add(col(VPE, i), col(VAR, i), EPS)),
             reads=(f"ln{i}:var",), writes=(f"ln{i}:vpe",))
    for i in range(2):
        E.op("scalar", (lambda s, i=i: s.activation(col(SD, i), col(VPE, i), AF.Sqrt)),
             reads=(f"ln{i}:vpe",), writes=(f"ln{i}:sd",))
    for i in range(2):
        E.op("vector", (lambda v, i=i: v.reciprocal(col(R0, i), col(SD, i))),
             reads=(f"ln{i}:sd",), writes=(f"ln{i}:r0",))
        E.op("vector", (lambda v, i=i: v.tensor_mul(col(RR, i), col(R0, i), col(R0, i))),
             reads=(f"ln{i}:r0",), writes=(f"ln{i}:rr",))
        E.op("vector", (lambda v, i=i: v.tensor_mul(col(H, i), col(RR, i), col(VPE, i))),
             reads=(f"ln{i}:rr", f"ln{i}:vpe"), writes=(f"ln{i}:h",))
        E.op("vector", (lambda v, i=i: v.tensor_scalar(col(H2, i), col(H, i), -0.5, 1.5,
                                                       ALU.mult, ALU.add)),
             reads=(f"ln{i}:h",), writes=(f"ln{i}:h2",))
        E.op("vector", (lambda v, i=i: v.tensor_mul(col(RSTD, i), col(R0, i), col(H2, i))),
             reads=(f"ln{i}:r0", f"ln{i}:h2"), writes=(f"ln{i}:rstd",))
        E.op("vector", (lambda v, i=i: v.tensor_mul(col(MR, i), col(MEAN, i), col(RSTD, i))),
             reads=(f"ln{i}:mean", f"ln{i}:rstd"), writes=(f"ln{i}:mr",))
        E.op("vector", (lambda v, i=i: v.tensor_scalar_mul(col(NMR, i), col(MR, i), -1.0)),
             reads=(f"ln{i}:mr",), writes=(f"ln{i}:nmr",))
    for i in range(2):
        j = 2 * p + i
        z, zk = z1b[i], zkeys[i]
        E.op("scalar",
             (lambda s, z=z, i=i: s.activation(y1b[:], z[:], AF.Identity,
                                               bias=col(NMR, i), scale=col(RSTD, i))),
             reads=(zk, f"ln{i}:nmr", f"ln{i}:rstd"), writes=("y1b",))
        E.op("sync",
             (lambda s, j=j: s.dma_start(
                 out=d_y1[j * 128:(j + 1) * 128, :], in_=y1b[:])),
             reads=("y1b",), writes=(f"y1d:{j}",), dma="y1w")


def _ln_norm(E, nc, z, zkey, yout, ykey, lns, sq_scratch, sqkey):
    S1, SSQ, MEAN, MSQ, M2, VAR, VPE, SD, R0, RR, H, H2, RSTD, MR, NMR = range(15)

    def col(c):
        return lns[:, c:c + 1]

    E.op("vector", lambda v: v.reduce_sum(col(S1), z[:], axis=AX.X),
         reads=(zkey,), writes=("ln:s1",))
    E.op("scalar",
         (lambda s: s.activation(sq_scratch[:, 0:D], z[:], AF.Square,
                                 accum_out=col(SSQ))),
         reads=(zkey,), writes=(sqkey, "ln:ssq"))
    E.op("vector", lambda v: v.tensor_scalar_mul(col(MEAN), col(S1), 1.0 / D),
         reads=("ln:s1",), writes=("ln:mean",))
    E.op("vector", lambda v: v.tensor_scalar_mul(col(MSQ), col(SSQ), 1.0 / D),
         reads=("ln:ssq",), writes=("ln:msq",))
    E.op("vector", lambda v: v.tensor_mul(col(M2), col(MEAN), col(MEAN)),
         reads=("ln:mean",), writes=("ln:m2",))
    E.op("vector", lambda v: v.tensor_sub(col(VAR), col(MSQ), col(M2)),
         reads=("ln:msq", "ln:m2"), writes=("ln:var",))
    E.op("vector", lambda v: v.tensor_scalar_add(col(VPE), col(VAR), EPS),
         reads=("ln:var",), writes=("ln:vpe",))
    E.op("scalar", lambda s: s.activation(col(SD), col(VPE), AF.Sqrt),
         reads=("ln:vpe",), writes=("ln:sd",))
    E.op("vector", lambda v: v.reciprocal(col(R0), col(SD)),
         reads=("ln:sd",), writes=("ln:r0",))
    E.op("vector", lambda v: v.tensor_mul(col(RR), col(R0), col(R0)),
         reads=("ln:r0",), writes=("ln:rr",))
    E.op("vector", lambda v: v.tensor_mul(col(H), col(RR), col(VPE)),
         reads=("ln:rr", "ln:vpe"), writes=("ln:h",))
    E.op("vector", lambda v: v.tensor_scalar(col(H2), col(H), -0.5, 1.5,
                                             ALU.mult, ALU.add),
         reads=("ln:h",), writes=("ln:h2",))
    E.op("vector", lambda v: v.tensor_mul(col(RSTD), col(R0), col(H2)),
         reads=("ln:r0", "ln:h2"), writes=("ln:rstd",))
    E.op("vector", lambda v: v.tensor_mul(col(MR), col(MEAN), col(RSTD)),
         reads=("ln:mean", "ln:rstd"), writes=("ln:mr",))
    E.op("vector", lambda v: v.tensor_scalar_mul(col(NMR), col(MR), -1.0),
         reads=("ln:mr",), writes=("ln:nmr",))
    E.op("scalar",
         (lambda s: s.activation(yout[:], z[:], AF.Identity,
                                 bias=col(NMR), scale=col(RSTD))),
         reads=(zkey, "ln:nmr", "ln:rstd"), writes=(ykey,))


_NC_CACHE = None


def _get_nc():
    global _NC_CACHE
    if _NC_CACHE is None:
        _NC_CACHE = build_nc()
    return _NC_CACHE


def kernel(L0_attn_x, L0_attn_Ws, L0_attn_Wa, L0_attn_Wo, L0_attn_bo,
           L0_ffn_W1, L0_ffn_W2, L0_ffn_W3):
    x = np.asarray(L0_attn_x, dtype=np.float32)
    Ws = np.asarray(L0_attn_Ws, dtype=np.float32)
    Wa = np.asarray(L0_attn_Wa, dtype=np.float32)
    Wo = np.asarray(L0_attn_Wo, dtype=np.float32)
    bo = np.asarray(L0_attn_bo, dtype=np.float32).reshape(1, D)
    W1 = np.asarray(L0_ffn_W1, dtype=np.float32)
    W2 = np.asarray(L0_ffn_W2, dtype=np.float32)
    W3 = np.asarray(L0_ffn_W3, dtype=np.float32)

    WoT = np.ascontiguousarray(Wo.T)
    def tile_up(W_):
        t = W_.T.reshape(16, 128, 64, 128)
        return np.ascontiguousarray(t.transpose(2, 1, 0, 3)).reshape(64, 128, 16 * 128)
    w1tp = tile_up(W1)
    w3tp = tile_up(W3)
    w2ch = np.ascontiguousarray(
        W2.reshape(8, 256, 64, 128).transpose(0, 3, 2, 1))

    H128 = _hadamard(128)
    I2H64 = np.kron(np.eye(2, dtype=np.float32), _hadamard(64)).astype(np.float32)
    ident = np.eye(128, dtype=np.float32)
    qi, ki = np.arange(128)[:, None], np.arange(128)[None, :]
    tri = np.where(ki <= qi, 0.0, NEG).astype(np.float32)
    mask_p = [np.concatenate([np.zeros((128, 128), np.float32), tri], axis=1),
              np.concatenate([tri, np.full((128, 128), NEG, np.float32)], axis=1)]

    nc = _get_nc()
    in_maps = []
    perms = []
    for c in range(8):
        b, P = c // 2, c % 2
        tiles = [31 - 2 * j - P for j in range(NSLOT)]
        perm = np.concatenate([np.arange(t * 128, (t + 1) * 128) for t in tiles])
        perms.append((b, perm))
        xb = x[b]
        xbT = np.ascontiguousarray(xb.T)
        in_maps.append({
            "xqT": np.ascontiguousarray(xbT[:, perm]),
            "xq": np.ascontiguousarray(xb[perm]),
            "xkT": xbT,
            "xk": np.ascontiguousarray(xb),
            "ws": Ws, "wa": Wa, "wot": WoT, "bo": bo,
            "w1tp": w1tp, "w3tp": w3tp, "w2ch": w2ch,
            "h128": H128, "i2h64": I2H64, "ident": ident,
            "mask": mask_p[P],
        })

    res = run_bass_kernel_spmd(nc, in_maps, core_ids=list(range(8)))
    out = np.empty((B, L, D), dtype=np.float32)
    for c in range(8):
        b, perm = perms[c]
        out[b][perm] = res.results[c]["out"]
    return out


# version 35
# speedup vs baseline: 1.1647x; 1.0118x over previous
import numpy as np
from contextlib import ExitStack

import concourse.bass as bass
import concourse.mybir as mybir
from concourse.bass_utils import run_bass_kernel_spmd

F32 = mybir.dt.float32
BF16 = mybir.dt.bfloat16
AF = mybir.ActivationFunctionType
ALU = mybir.AluOpType
AX = mybir.AxisListType

B, L, D, R, HF = 4, 4096, 2048, 256, 8192
TQ = 2048
NSLOT = 16
CJ = [32 - 2 * j for j in range(NSLOT)]
NGRP = 4
GSL = 4
GMAX = [CJ[GSL * g] for g in range(NGRP)]
FT = 1024
NEG = -1.0e30
EPS = 1.0e-5
INV16 = 1.0 / 16.0
FWSCALE = float(1.0 / np.sqrt(HF))

ENGINES = ("tensor", "vector", "scalar", "sync", "gpsimd")


class _Op:
    __slots__ = ("eng", "fn", "deps", "marked", "value", "is_dma", "sem_key",
                 "dma_waits", "snap")

    def __init__(self):
        self.deps = set()
        self.marked = False
        self.value = 0
        self.is_dma = False
        self.sem_key = None
        self.dma_waits = {}


class Emitter:
    def __init__(self):
        self.eng_ops = {e: [] for e in ENGINES}
        self.writer = {}
        self.readers = {}
        self.barrier_ops = []
        self.dma_counts = {}

    def op(self, eng, fn, reads=(), writes=(), dma=None):
        o = _Op()
        o.eng = eng
        o.fn = fn
        if dma is not None:
            o.is_dma = True
            o.sem_key = dma
        deps = o.deps
        for k in reads:
            w = self.writer.get(k)
            if w is not None:
                deps.add(w)
        for k in writes:
            rs = self.readers.get(k)
            if rs:
                deps.update(rs)
            w = self.writer.get(k)
            if w is not None:
                deps.add(w)
        deps.update(self.barrier_ops)
        deps.discard(o)
        if eng == "tensor":
            for p in [p for p in deps if p.eng == "tensor" and not p.is_dma]:
                deps.discard(p)
        for p in list(deps):
            if p.is_dma:
                v = 16 * self.dma_counts[p.sem_key]
                if o.dma_waits.get(p.sem_key, 0) < v:
                    o.dma_waits[p.sem_key] = v
                deps.discard(p)
            else:
                p.marked = True
        if dma is not None:
            self.dma_counts[dma] = self.dma_counts.get(dma, 0) + 1
        for k in reads:
            self.readers.setdefault(k, []).append(o)
        for k in writes:
            self.writer[k] = o
            self.readers[k] = []
        self.eng_ops[eng].append(o)
        return o

    def barrier(self):
        self.barrier_ops = [lst[-1] for lst in self.eng_ops.values() if lst]
        for p in self.barrier_ops:
            if p.is_dma:
                pass
            else:
                p.marked = True

    def finalize(self):
        for e in ("tensor", "vector", "scalar"):
            v = 0
            for o in self.eng_ops[e]:
                if o.marked:
                    v += 1
                    o.value = v

    def replay(self, eng_name, handle, eng_sems, dma_sems):
        waited = {}
        for o in self.eng_ops[eng_name]:
            waits = {}
            for p in o.deps:
                if p.is_dma:
                    continue
                key = ("eng", p.eng)
                if waits.get(key, 0) < p.value:
                    waits[key] = p.value
            for sk, v in o.dma_waits.items():
                key = ("dma", sk)
                if waits.get(key, 0) < v:
                    waits[key] = v
            for key, v in waits.items():
                if waited.get(key, 0) >= v:
                    continue
                waited[key] = v
                sem = eng_sems[key[1]] if key[0] == "eng" else dma_sems[key[1]]
                handle.wait_ge(sem, v)
            ins = o.fn(handle)
            if o.is_dma:
                ins.then_inc(dma_sems[o.sem_key], 16)
            elif o.marked:
                ins.then_inc(eng_sems[o.eng], 1)


def _hadamard(n):
    h = np.array([[1.0]], dtype=np.float32)
    while h.shape[0] < n:
        h = np.block([[h, h], [h, -h]]).astype(np.float32)
    return h


def build_nc():
    nc = bass.Bass("TRN2", target_bir_lowering=False)
    E = Emitter()

    d_xqT = nc.dram_tensor("xqT", [D, TQ], F32, kind="ExternalInput").ap()
    d_xq = nc.dram_tensor("xq", [TQ, D], F32, kind="ExternalInput").ap()
    d_xkT = nc.dram_tensor("xkT", [D, L], F32, kind="ExternalInput").ap()
    d_xk = nc.dram_tensor("xk", [L, D], F32, kind="ExternalInput").ap()
    d_ws = nc.dram_tensor("ws", [D, R], F32, kind="ExternalInput").ap()
    d_wa = nc.dram_tensor("wa", [D, R], F32, kind="ExternalInput").ap()
    d_wot = nc.dram_tensor("wot", [D, D], F32, kind="ExternalInput").ap()
    d_bo = nc.dram_tensor("bo", [1, D], F32, kind="ExternalInput").ap()
    d_w1 = nc.dram_tensor("w1tp", [64, 128, 16 * 128], F32, kind="ExternalInput").ap()
    d_w3 = nc.dram_tensor("w3tp", [64, 128, 16 * 128], F32, kind="ExternalInput").ap()
    d_w2 = nc.dram_tensor("w2ch", [8, 128, 64, 256], F32, kind="ExternalInput").ap()
    d_h128 = nc.dram_tensor("h128", [128, 128], F32, kind="ExternalInput").ap()
    d_i2h64 = nc.dram_tensor("i2h64", [128, 128], F32, kind="ExternalInput").ap()
    d_ident = nc.dram_tensor("ident", [128, 128], F32, kind="ExternalInput").ap()
    d_mask = nc.dram_tensor("mask", [128, 256], F32, kind="ExternalInput").ap()
    d_out = nc.dram_tensor("out", [TQ, D], F32, kind="ExternalOutput").ap()
    d_y1 = nc.dram_tensor("y1d", [TQ, D], F32, kind="Internal").ap()
    d_zz = nc.dram_tensor("zz", [TQ, D], F32, kind="Internal").ap()
    d_vb = nc.dram_tensor("vb", [L, D], BF16, kind="Internal").ap()
    d_hid = nc.dram_tensor("hid", [64, 128, TQ], BF16, kind="Internal").ap()
    d_tmp = nc.dram_tensor("w2tmp", [8, 128, 64, 256], BF16, kind="Internal").ap()
    d_w2p = nc.dram_tensor("w2p", [64, 128, D], BF16, kind="Internal").ap()

    with ExitStack() as S0:
        C = S0.enter_context
        ident_bf = C(nc.sbuf_tensor("ident_bf", [128, 128], BF16))
        h128_bf = C(nc.sbuf_tensor("h128_bf", [128, 128], BF16))
        i2h64_bf = C(nc.sbuf_tensor("i2h64_bf", [128, 128], BF16))
        mask_sb = C(nc.sbuf_tensor("mask_sb", [128, 256], F32))
        bo_bf = C(nc.sbuf_tensor("bo_bf", [1, D], BF16))
        ones1 = C(nc.sbuf_tensor("ones1", [1, 128], BF16))
        lns = C(nc.sbuf_tensor("lns", [128, 32], F32))
        nmax = C(nc.sbuf_tensor("nmax", [128, 1], F32))
        sums = C(nc.sbuf_tensor("sums", [128, 1], F32))
        inv1 = C(nc.sbuf_tensor("inv1", [128, 1], F32))

        E.op("gpsimd", lambda g: g.dma_start(out=ident_bf[:], in_=d_ident),
             writes=("ident_bf",), dma="ident")
        E.op("gpsimd", lambda g: g.dma_start(out=h128_bf[:], in_=d_h128),
             writes=("h128_bf",), dma="h128")
        E.op("gpsimd", lambda g: g.dma_start(out=i2h64_bf[:], in_=d_i2h64),
             writes=("i2h64_bf",), dma="i2h64")
        E.op("gpsimd", lambda g: g.dma_start(out=bo_bf[:], in_=d_bo),
             writes=("bo_bf",), dma="bold")
        E.op("sync", lambda s: s.dma_start(out=mask_sb[:], in_=d_mask),
             writes=("mask_sb",), dma="maskld")
        E.op("vector", lambda v: v.memset(ones1[:], 1.0), writes=("ones1",))

        DC = 256
        NCH = D // DC
        d_tmp_s = d_tmp.rearrange("ch (s l) hi d -> ch s hi l d", s=2)
        d_w2p_s = d_w2p.rearrange("c (s l) t -> s c l t", s=2)
        with ExitStack() as ST:
            CT = ST.enter_context
            rhs1 = [CT(nc.sbuf_tensor(f"rhs1_{i}", [128, 64, DC], BF16)) for i in range(2)]
            t2sb = [CT(nc.sbuf_tensor(f"t2sb_{i}", [128, 64, DC], BF16)) for i in range(2)]
            t1full = [CT(nc.sbuf_tensor(f"t1full_{i}", [128, 64, DC], BF16)) for i in range(1)]
            t2full = [CT(nc.sbuf_tensor(f"t2full_{i}", [128, 64, DC], BF16)) for i in range(1)]
            pt1 = [CT(nc.psum_tensor(f"pt1_{i}", [128, 512], F32)) for i in range(4)]
            pt2 = [CT(nc.psum_tensor(f"pt2_{i}", [128, 512], F32)) for i in range(4)]
            NSC = (64 * DC) // 512

            def emit_load(ch):
                ri = ch % len(rhs1)
                buf = rhs1[ri]
                E.op("gpsimd",
                     (lambda g, ch=ch, buf=buf: g.dma_start(
                         out=buf[:], in_=d_w2[ch])),
                     writes=(f"rhs1:{ri}",), dma=f"w2in{ch % 2}")

            def emit_vb(ch):
                E.op("gpsimd",
                     (lambda g, ch=ch: g.dma_start(
                         out=d_vb[ch * 128:(ch + 1) * 128, :],
                         in_=d_xk[ch * 128:(ch + 1) * 128, :])),
                     writes=(f"vb:{ch}",), dma=f"vbs{ch // 8}")

            def emit_T1(ch):
                ri = ch % len(rhs1)
                buf = rhs1[ri]
                rflat = buf.rearrange("p a b -> p (a b)")
                t1b = t1full[0]
                t1flat = t1b.rearrange("p a b -> p (a b)")
                for c in range(NSC):
                    ps = pt1[c % 4]
                    E.op("tensor",
                         (lambda t, c=c, ps=ps, rflat=rflat:
                          t.matmul(ps[:], h128_bf[:],
                                   rflat[:, c * 512:(c + 1) * 512],
                                   start=True, stop=True,
                                   skip_group_check=True)),
                         reads=(f"rhs1:{ri}", "h128_bf"),
                         writes=(f"pt1{c % 4}",))
                    if c % 2 == 0:
                        E.op("vector",
                             (lambda h, c=c, ps=ps, t1flat=t1flat: h.tensor_scalar(
                                 t1flat[:, c * 512:(c + 1) * 512],
                                 ps[:], FWSCALE, None, ALU.mult)),
                             reads=(f"pt1{c % 4}",), writes=(f"t1f:0:{c}",))
                    else:
                        E.op("scalar",
                             (lambda h, c=c, ps=ps, t1flat=t1flat: h.activation(
                                 t1flat[:, c * 512:(c + 1) * 512],
                                 ps[:], AF.Copy, scale=FWSCALE)),
                             reads=(f"pt1{c % 4}",), writes=(f"t1f:0:{c}",))
                E.op("sync",
                     (lambda s, ch=ch, t1b=t1b: s.dma_start(
                         out=d_tmp[ch], in_=t1b[:])),
                     reads=tuple(f"t1f:0:{c}" for c in range(NSC)),
                     writes=(f"tmpD:{ch}",), dma=f"tmpw{ch % 2}")

            def emit_T2(ch):
                d0 = ch * DC
                buf = t2sb[ch % 2]
                rflat = buf.rearrange("p a b -> p (a b)")
                t2b = t2full[0]
                t2flat = t2b.rearrange("p a b -> p (a b)")
                for sh in range(2):
                    E.op("sync",
                         (lambda s, ch=ch, buf=buf, sh=sh: s.dma_start(
                             out=buf[sh * 64:(sh + 1) * 64, :, :],
                             in_=d_tmp_s[ch, sh])),
                         reads=(f"tmpD:{ch}",),
                         writes=(f"t2sb:{ch % 2}",), dma=f"tmpr{ch % 2}")
                for c in range(NSC):
                    ps = pt2[c % 4]
                    E.op("tensor",
                         (lambda t, c=c, ps=ps, rflat=rflat:
                          t.matmul(ps[:], i2h64_bf[:],
                                   rflat[:, c * 512:(c + 1) * 512],
                                   start=True, stop=True,
                                   skip_group_check=True)),
                         reads=(f"t2sb:{ch % 2}", "i2h64_bf"),
                         writes=(f"pt2{c % 4}",))
                    if c % 2 == 0:
                        E.op("vector",
                             (lambda h, c=c, ps=ps, t2flat=t2flat: h.tensor_copy(
                                 t2flat[:, c * 512:(c + 1) * 512], ps[:])),
                             reads=(f"pt2{c % 4}",), writes=(f"t2f:0:{c}",))
                    else:
                        E.op("scalar",
                             (lambda h, c=c, ps=ps, t2flat=t2flat: h.activation(
                                 t2flat[:, c * 512:(c + 1) * 512], ps[:], AF.Copy)),
                             reads=(f"pt2{c % 4}",), writes=(f"t2f:0:{c}",))
                for sh in range(2):
                    E.op("sync",
                         (lambda s, ch=ch, d0=d0, sh=sh, t2b=t2b: s.dma_start(
                             out=d_w2p_s[sh, :, :, d0:d0 + DC],
                             in_=t2b[sh * 64:(sh + 1) * 64, :, :])),
                         reads=tuple(f"t2f:0:{c}" for c in range(NSC)),
                         writes=(f"w2p:{ch}",), dma="w2pw")

            emit_load(0)
            emit_load(1)
            for ch in range(NCH):
                emit_T1(ch)
                if ch + 2 < NCH:
                    emit_load(ch + 2)
                for q in range(4):
                    emit_vb(4 * ch + q)
                if ch >= 2:
                    emit_T2(ch - 2)
            emit_T2(NCH - 2)
            emit_T2(NCH - 1)
            E.barrier()

        SA = ExitStack()
        CA = SA.enter_context
        st_bf = [CA(nc.sbuf_tensor(f"st{r}", [128, TQ], BF16)) for r in range(2)]
        ac_bf = [CA(nc.sbuf_tensor(f"ac{r}", [128, L], BF16)) for r in range(2)]
        with ExitStack() as S1:
            C1 = S1.enter_context
            ws_bf = C1(nc.sbuf_tensor("ws_bf", [128, 16 * R], BF16))
            wa_bf = C1(nc.sbuf_tensor("wa_bf", [128, 16 * R], BF16))
            xsb = [C1(nc.sbuf_tensor(f"xsb{i}", [128, 2048], BF16)) for i in range(4)]
            pj = [C1(nc.psum_tensor(f"pj{i}", [128, 512], F32)) for i in range(8)]

            for k in range(16):
                E.op("gpsimd",
                     (lambda g, k=k: g.dma_start(
                         out=ws_bf[:, k * R:(k + 1) * R],
                         in_=d_ws[k * 128:(k + 1) * 128, :])),
                     writes=(f"ws:{k}",), dma="wsld")
                E.op("gpsimd",
                     (lambda g, k=k: g.dma_start(
                         out=wa_bf[:, k * R:(k + 1) * R],
                         in_=d_wa[k * 128:(k + 1) * 128, :])),
                     writes=(f"wa:{k}",), dma="wald")

            def proj(which, wbuf, wkey, src, src_col0, dst, dst_col0):
                for k in range(16):
                    buf = xsb[k % 4]
                    E.op("gpsimd",
                         (lambda g, k=k, buf=buf: g.dma_start(
                             out=buf[:],
                             in_=src[k * 128:(k + 1) * 128,
                                     src_col0:src_col0 + 2048])),
                         reads=(), writes=(f"xsb{k % 4}",), dma=f"xsb{k % 4}")
                    for m in range(2):
                        for n in range(4):
                            p = pj[m * 4 + n]
                            E.op("tensor",
                                 (lambda t, k=k, m=m, n=n, p=p, buf=buf:
                                  t.matmul(p[:],
                                           wbuf[:, k * R + m * 128:k * R + (m + 1) * 128],
                                           buf[:, n * 512:(n + 1) * 512],
                                           start=(k == 0), stop=(k == 15),
                                           skip_group_check=True)),
                                 reads=(f"{wkey}:{k}", f"xsb{k % 4}"),
                                 writes=(f"pj{m * 4 + n}",))
                for m in range(2):
                    for n in range(4):
                        p = pj[m * 4 + n]
                        scale = INV16 if which == "state" else None
                        if scale is not None:
                            E.op("vector",
                                 (lambda v, m=m, n=n, p=p:
                                  v.tensor_scalar(
                                      dst[m][:, dst_col0 + n * 512:dst_col0 + (n + 1) * 512],
                                      p[:], INV16, None, ALU.mult)),
                                 reads=(f"pj{m * 4 + n}",),
                                 writes=(f"{'st' if which == 'state' else 'ac'}:{m}:{dst_col0 + n * 512}",))
                        else:
                            E.op("vector",
                                 (lambda v, m=m, n=n, p=p:
                                  v.tensor_copy(
                                      dst[m][:, dst_col0 + n * 512:dst_col0 + (n + 1) * 512],
                                      p[:])),
                                 reads=(f"pj{m * 4 + n}",),
                                 writes=(f"ac:{m}:{dst_col0 + n * 512}",))

            proj("state", ws_bf, "ws", d_xqT, 0, st_bf, 0)
            proj("action", wa_bf, "wa", d_xkT, 0, ac_bf, 0)
            proj("action", wa_bf, "wa", d_xkT, 2048, ac_bf, 2048)
            E.barrier()

        with ExitStack() as S2:
            C2 = S2.enter_context
            wot_bf = C2(nc.sbuf_tensor("wot_bf", [128, 16 * D], BF16))
            Praw = C2(nc.sbuf_tensor("Praw", [128, 4096], F32))
            Pbf2 = [C2(nc.sbuf_tensor(f"Pbf{i}", [128, 4096], BF16)) for i in range(2)]
            PT2 = [C2(nc.sbuf_tensor(f"PT{i}", [128, 32 * 256], BF16)) for i in range(2)]
            atT = C2(nc.sbuf_tensor("atT", [128, 16 * 256], BF16))
            v_c = [C2(nc.sbuf_tensor(f"vc{i}", [128, 1024], BF16)) for i in range(4)]
            z1b = [C2(nc.sbuf_tensor(f"z1b{i}", [128, D], F32)) for i in range(2)]
            y1b = C2(nc.sbuf_tensor("y1b", [128, D], F32))
            xqt = C2(nc.sbuf_tensor("xqt", [128, D], F32))
            ptps = [C2(nc.psum_tensor(f"ptps{i}", [128, 128], BF16)) for i in range(2)]
            sc = [C2(nc.psum_tensor(f"sc{i}", [128, 512], F32)) for i in range(2)]
            av = [C2(nc.psum_tensor(f"av{i}", [128, 512], F32)) for i in range(4)]

            for k in range(16):
                E.op("gpsimd",
                     (lambda g, k=k: g.dma_start(
                         out=wot_bf[:, k * D:(k + 1) * D],
                         in_=d_wot[k * 128:(k + 1) * 128, :])),
                     writes=(f"wot:{k}",), dma="wotld")

            def emit_scores(p, i):
                j = 2 * p + i
                cj = CJ[j]
                W = cj * 128
                Pbf = Pbf2[i]
                pbk = f"pbf{i}"
                off = 0
                bi = 0
                while off < W:
                    wdt = min(512, W - off)
                    ps = sc[bi % 2]
                    for r in range(2):
                        E.op("tensor",
                             (lambda t, r=r, j=j, off=off, wdt=wdt, ps=ps:
                              t.matmul(ps[:, 0:wdt],
                                       st_bf[r][:, j * 128:(j + 1) * 128],
                                       ac_bf[r][:, off:off + wdt],
                                       start=(r == 0), stop=(r == 1))),
                             reads=(f"st:{r}:{(j * 128) // 512 * 512}",
                                    f"ac:{r}:{(off // 512) * 512}",
                                    f"ac:{r}:{((off + wdt - 1) // 512) * 512}"),
                             writes=(f"scp{bi % 2}",))
                    if off + wdt == W:
                        if wdt == 512:
                            E.op("vector",
                                 (lambda v, off=off, ps=ps:
                                  v.tensor_copy(Praw[:, off:off + 256], ps[:, 0:256])),
                                 reads=(f"scp{bi % 2}",), writes=("praw",))
                            E.op("vector",
                                 (lambda v, off=off, ps=ps:
                                  v.tensor_tensor(Praw[:, off + 256:off + 512],
                                                  ps[:, 256:512], mask_sb[:], ALU.add)),
                                 reads=(f"scp{bi % 2}", "mask_sb"), writes=("praw",))
                        else:
                            E.op("vector",
                                 (lambda v, off=off, ps=ps:
                                  v.tensor_tensor(Praw[:, off:off + 256],
                                                  ps[:, 0:256], mask_sb[:], ALU.add)),
                                 reads=(f"scp{bi % 2}", "mask_sb"), writes=("praw",))
                    else:
                        E.op("vector",
                             (lambda v, off=off, wdt=wdt, ps=ps:
                              v.tensor_copy(Praw[:, off:off + wdt], ps[:, 0:wdt])),
                             reads=(f"scp{bi % 2}",), writes=("praw",))
                    off += wdt
                    bi += 1
                E.op("vector",
                     (lambda v, W=W: v.reduce_max(nmax[:], Praw[:, 0:W],
                                                  axis=AX.X, negate=True)),
                     reads=("praw",), writes=("nmax",))
                E.op("scalar",
                     (lambda s, W=W, Pbf=Pbf: s.activation(
                         Pbf[:, 0:W], Praw[:, 0:W], AF.Exp, bias=nmax[:],
                         accum_out=sums[:])),
                     reads=("praw", "nmax"), writes=(pbk, "sums"))
                E.op("vector", lambda v: v.reciprocal(inv1[:], sums[:]),
                     reads=("sums",), writes=("inv1",))
                E.op("vector",
                     (lambda v, W=W, Pbf=Pbf: v.tensor_scalar_mul(
                         Pbf[:, 0:W], Pbf[:, 0:W], inv1[:])),
                     reads=(pbk, "inv1"), writes=(pbk,))

            def emit_av_memsets(p, half):
                for b in range(4):
                    E.op("vector",
                         (lambda v, b=b: v.memset(av[b][:], 0.0)),
                         writes=(f"av{b}",))

            def emit_av_half(p, half, inter=None):
                maxp = CJ[2 * p]
                for ch in range(maxp):
                    buf = v_c[ch % 4]
                    E.op("sync",
                         (lambda g2, ch=ch, half=half, buf=buf: g2.dma_start(
                             out=buf[:],
                             in_=d_vb[ch * 128:(ch + 1) * 128,
                                      half * 1024:(half + 1) * 1024])),
                         reads=(f"vb:{ch}",),
                         writes=(f"vc{ch % 4}",), dma=f"vc{ch % 4}")
                    for m in range(8):
                        E.op("tensor",
                             (lambda t, ch=ch, m=m, maxp=maxp, buf=buf, p=p:
                              t.matmul(av[m // 2][:, (m % 2) * 256:(m % 2) * 256 + 256],
                                       buf[:, m * 128:(m + 1) * 128],
                                       PT2[p % 2][:, ch * 256:(ch + 1) * 256],
                                       start=False, stop=(ch == maxp - 1),
                                       skip_group_check=True)),
                             reads=(f"vc{ch % 4}", f"pt:{ch}:0", f"pt:{ch}:1"),
                             writes=(f"av{m // 2}",))
                    if inter:
                        n_per = max(1, len(inter) // maxp + 1)
                        for _ in range(n_per):
                            if inter:
                                inter.pop(0)()
                for m in range(8):
                    hc = half * 8 + m
                    E.op("vector",
                         (lambda v, hc=hc, m=m:
                          v.tensor_copy(atT[:, hc * 256:(hc + 1) * 256],
                                        av[m // 2][:, (m % 2) * 256:(m % 2) * 256 + 256])),
                         reads=(f"av{m // 2}",), writes=(f"atT:{hc}",))
                if inter:
                    while inter:
                        inter.pop(0)()

            def pt_thunks(p):
                maxp = CJ[2 * p]
                thunks = []
                for i in range(2):
                    cj = CJ[2 * p + i]
                    Pbf = Pbf2[i]
                    pbk = f"pbf{i}"
                    for kb in range(cj):
                        pp = ptps[(len(thunks)) % 2]
                        def th(kb=kb, i=i, pp=pp, Pbf=Pbf, pbk=pbk, p=p,
                               ppk=f"ptps{len(thunks) % 2}"):
                            E.op("tensor",
                                 (lambda t: t.transpose(
                                     pp[:], Pbf[:, kb * 128:(kb + 1) * 128],
                                     ident_bf[:])),
                                 reads=(pbk, "ident_bf"), writes=(ppk,))
                            E.op("vector",
                                 (lambda v: v.tensor_copy(
                                     PT2[p % 2][:, kb * 256 + i * 128:kb * 256 + (i + 1) * 128],
                                     pp[:])),
                                 reads=(ppk,), writes=(f"pt:{kb}:{i}",))
                        thunks.append(th)
                    for kb in range(cj, maxp):
                        def thz(kb=kb, i=i, p=p):
                            E.op("vector",
                                 (lambda v: v.memset(
                                     PT2[p % 2][:, kb * 256 + i * 128:kb * 256 + (i + 1) * 128], 0.0)),
                                 writes=(f"pt:{kb}:{i}",))
                        thunks.append(thz)
                return thunks

            def emit_wo_epi(p):
                for i in range(2):
                    j = 2 * p + i
                    E.op("sync",
                         (lambda s, j=j: s.dma_start(
                             out=xqt[:], in_=d_xq[j * 128:(j + 1) * 128, :])),
                         writes=("xqt",), dma="xqtld")
                    z1 = z1b[i]
                    zkey = f"z1b{i}"
                    for n in range(4):
                        ps = sc[n % 2]
                        for k in range(16):
                            E.op("tensor",
                                 (lambda t, k=k, n=n, i=i, ps=ps:
                                  t.matmul(ps[:],
                                           atT[:, k * 256 + i * 128:k * 256 + (i + 1) * 128],
                                           wot_bf[:, k * D + n * 512:k * D + (n + 1) * 512],
                                           start=(k == 0), stop=False,
                                           skip_group_check=True)),
                                 reads=(f"atT:{k}", f"wot:{k}"),
                                 writes=(f"scp{n % 2}",))
                        E.op("tensor",
                             (lambda t, n=n, ps=ps:
                              t.matmul(ps[:], ones1[:],
                                       bo_bf[:, n * 512:(n + 1) * 512],
                                       start=False, stop=True,
                                       skip_group_check=True)),
                             reads=("ones1", "bo_bf"), writes=(f"scp{n % 2}",))
                        E.op("vector",
                             (lambda v, n=n, ps=ps, z1=z1:
                              v.tensor_tensor(z1[:, n * 512:(n + 1) * 512], ps[:],
                                              xqt[:, n * 512:(n + 1) * 512], ALU.add)),
                             reads=(f"scp{n % 2}", "xqt"), writes=(zkey,))
                _ln_pair(E, z1b, ("z1b0", "z1b1"), y1b, lns, Praw, d_y1, p)

            order = [7, 6, 5, 4, 3, 2, 1, 0]
            for step in range(9):
                prev = order[step - 1] if step >= 1 else None
                cur = order[step] if step < 8 else None
                if prev is not None:
                    emit_av_memsets(prev, 0)
                if cur is not None:
                    emit_scores(cur, 0)
                    emit_scores(cur, 1)
                if prev is not None:
                    emit_av_half(prev, 0)
                    emit_av_memsets(prev, 1)
                    emit_av_half(prev, 1,
                                 inter=(pt_thunks(cur) if cur is not None else None))
                    emit_wo_epi(prev)
                elif cur is not None:
                    for th in pt_thunks(cur):
                        th()
            E.barrier()
        SA.close()

        with ExitStack() as SB1:
            CB = SB1.enter_context
            y1t = CB(nc.sbuf_tensor("y1t", [128, 16 * TQ], BF16))
            w1b = [CB(nc.sbuf_tensor(f"w1b{i}", [128, 16 * 128], BF16)) for i in range(3)]
            w3b = [CB(nc.sbuf_tensor(f"w3b{i}", [128, 16 * 128], BF16)) for i in range(3)]
            y1bf = [CB(nc.sbuf_tensor(f"y1bf{i}", [128, D], BF16)) for i in range(2)]
            hstg = [CB(nc.sbuf_tensor(f"hstg{i}", [128, 512], BF16)) for i in range(4)]
            silu = [CB(nc.sbuf_tensor(f"silu{i}", [128, 512], BF16)) for i in range(2)]
            trps = CB(nc.psum_tensor("trps", [128, 128], BF16))
            up = [CB(nc.psum_tensor(f"up{i}", [128, 512], F32)) for i in range(4)]

            for m in range(16):
                buf = y1bf[m % 2]
                E.op("gpsimd",
                     (lambda g2, m=m, buf=buf: g2.dma_start(
                         out=buf[:],
                         in_=d_y1[m * 128:(m + 1) * 128, :])),
                     reads=(f"y1d:{m}",),
                     writes=(f"y1bf{m % 2}",), dma=f"y1bf{m % 2}")
                for c in range(16):
                    E.op("tensor",
                         (lambda t, c=c, buf=buf, m=m:
                          t.transpose(trps[:], buf[:, c * 128:(c + 1) * 128],
                                      ident_bf[:])),
                         reads=(f"y1bf{m % 2}", "ident_bf"), writes=("trps",))
                    E.op("vector",
                         (lambda v, c=c, m=m:
                          v.tensor_copy(
                              y1t[:, c * TQ + m * 128:c * TQ + (m + 1) * 128],
                              trps[:])),
                         reads=("trps",), writes=(f"y1t:{c}:{m}",))
            for hc in range(64):
                bw1, bw3 = w1b[hc % 3], w3b[hc % 3]
                E.op("gpsimd",
                     (lambda g2, hc=hc, bw1=bw1: g2.dma_start(
                         out=bw1[:], in_=d_w1[hc])),
                     writes=(f"w1b{hc % 3}",), dma=f"w1b{hc % 3}")
                E.op("gpsimd",
                     (lambda g2, hc=hc, bw3=bw3: g2.dma_start(
                         out=bw3[:], in_=d_w3[hc])),
                     writes=(f"w3b{hc % 3}",), dma=f"w3b{hc % 3}")
                for n in range(4):
                    p1 = up[n % 2]
                    p3 = up[2 + n % 2]
                    for k in range(16):
                        E.op("tensor",
                             (lambda t, k=k, n=n, p1=p1, bw1=bw1:
                              t.matmul(p1[:],
                                       bw1[:, k * 128:(k + 1) * 128],
                                       y1t[:, k * TQ + n * 512:k * TQ + n * 512 + 512],
                                       start=(k == 0), stop=(k == 15),
                                       skip_group_check=True)),
                             reads=(f"w1b{hc % 3}",) + tuple(
                                 f"y1t:{k}:{n * 4 + mm}" for mm in range(4)),
                             writes=(f"up{n % 2}",))
                    for k in range(16):
                        E.op("tensor",
                             (lambda t, k=k, n=n, p3=p3, bw3=bw3:
                              t.matmul(p3[:],
                                       bw3[:, k * 128:(k + 1) * 128],
                                       y1t[:, k * TQ + n * 512:k * TQ + n * 512 + 512],
                                       start=(k == 0), stop=(k == 15),
                                       skip_group_check=True)),
                             reads=(f"w3b{hc % 3}",) + tuple(
                                 f"y1t:{k}:{n * 4 + mm}" for mm in range(4)),
                             writes=(f"up{2 + n % 2}",))
                    sl = silu[n % 2]
                    hb = hstg[n]
                    E.op("scalar",
                         (lambda s, p1=p1, sl=sl: s.activation(sl[:], p1[:], AF.Silu)),
                         reads=(f"up{n % 2}",), writes=(f"silu{n % 2}",))
                    E.op("vector",
                         (lambda v, p3=p3, sl=sl, hb=hb:
                          v.tensor_tensor(hb[:], sl[:], p3[:], ALU.mult)),
                         reads=(f"silu{n % 2}", f"up{2 + n % 2}"),
                         writes=(f"hstg{n}",))
                    E.op("sync",
                         (lambda s2, hc=hc, n=n, hb=hb: s2.dma_start(
                             out=d_hid[hc][:, n * 512:(n + 1) * 512], in_=hb[:])),
                         reads=(f"hstg{n}",), writes=(f"hid:{hc}:{n}",), dma="hidw")
            E.barrier()

        FT2 = 512
        with ExitStack() as SB2:
            CB2 = SB2.enter_context
            Hd2r = [CB2(nc.sbuf_tensor(f"Hd2_{i}", [128, 64 * FT2], BF16)) for i in range(2)]
            zt = CB2(nc.sbuf_tensor("zt", [128, D], F32))
            yt = CB2(nc.sbuf_tensor("yt", [128, D], F32))
            ot = CB2(nc.sbuf_tensor("ot", [128, D], F32))
            sqc = CB2(nc.sbuf_tensor("sqc", [128, D], BF16))
            w2h = [CB2(nc.sbuf_tensor(f"w2h{i}", [128, 8 * 512], BF16)) for i in range(4)]
            z2s = [CB2(nc.sbuf_tensor(f"z2s{i}", [128, 512], F32)) for i in range(2)]
            dps = [CB2(nc.psum_tensor(f"dps{i}", [128, 512], F32)) for i in range(8)]

            def emit_C(tc):
                for mc in range(tc * 4, tc * 4 + 4):
                    E.op("sync",
                         (lambda s, mc=mc: s.dma_start(
                             out=zt[:], in_=d_zz[mc * 128:(mc + 1) * 128, :])),
                         reads=tuple(f"zz:{mc // 4}:{mc % 4}:{n}" for n in range(4)),
                         writes=("zt",), dma="ztld")
                    E.op("gpsimd",
                         (lambda s, mc=mc: s.dma_start(
                             out=yt[:], in_=d_y1[mc * 128:(mc + 1) * 128, :])),
                         reads=(f"y1d:{mc}",), writes=("yt",), dma="ytld")
                    E.op("vector",
                         (lambda v: v.tensor_add(zt[:], zt[:], yt[:])),
                         reads=("zt", "yt"), writes=("zt",))
                    _ln_norm(E, nc, zt, "zt", ot, "ot", lns, sqc, "sqc")
                    E.op("sync",
                         (lambda s, mc=mc: s.dma_start(
                             out=d_out[mc * 128:(mc + 1) * 128, :], in_=ot[:])),
                         reads=("ot",), writes=(f"out:{mc}",), dma="outw")

            d_hid_r = d_hid.rearrange("c p t -> p c t")
            d_w2p_r = d_w2p.rearrange("c p t -> p c t")
            def emit_h2l(tt):
                Hd2 = Hd2r[tt % 2]
                for hc0 in range(0, 64, 8):
                    rd = tuple(f"hid:{hc}:{tt}" for hc in range(hc0, hc0 + 8))
                    E.op("gpsimd",
                         (lambda s2, hc0=hc0, tt=tt, Hd2=Hd2: s2.dma_start(
                             out=Hd2[:, hc0 * FT2:(hc0 + 8) * FT2],
                             in_=d_hid_r[:, hc0:hc0 + 8,
                                         tt * FT2:(tt + 1) * FT2])),
                         reads=rd, writes=(f"Hd2:{tt % 2}:{hc0}",), dma="h2l")

            emit_h2l(0)
            for tt in range(4):
                if tt + 1 < 4:
                    emit_h2l(tt + 1)
                if tt >= 1:
                    emit_C(tt - 1)
                Hd2 = Hd2r[tt % 2]
                hkey = f"Hd2:{tt % 2}"
                dbase = 4 * (tt % 2)
                for n in range(4):
                    for qb in range(8):
                        s_idx = n * 8 + qb
                        buf = w2h[s_idx % 4]
                        hh0 = qb * 8
                        E.op("gpsimd",
                             (lambda g2, hh0=hh0, n=n, buf=buf: g2.dma_start(
                                 out=buf[:],
                                 in_=d_w2p_r[:, hh0:hh0 + 8,
                                             n * 512:(n + 1) * 512])),
                             reads=tuple(f"w2p:{ch}"
                                         for ch in range(2 * n, 2 * n + 2)),
                             writes=(f"w2q{s_idx % 4}",),
                             dma=f"w2h{s_idx % 4}")
                        for m in range(4):
                            for h2 in range(8):
                                hh = hh0 + h2
                                E.op("tensor",
                                     (lambda t, m=m, h2=h2, hh=hh, qb=qb,
                                      buf=buf, Hd2=Hd2, dbase=dbase:
                                      t.matmul(dps[dbase + m][:],
                                               Hd2[:, hh * FT2 + m * 128:hh * FT2 + (m + 1) * 128],
                                               buf[:, h2 * 512:(h2 + 1) * 512],
                                               start=(qb == 0 and h2 == 0),
                                               stop=(qb == 7 and h2 == 7),
                                               skip_group_check=True)),
                                     reads=(f"w2q{s_idx % 4}", f"{hkey}:{hh0}"),
                                     writes=(f"dps{dbase + m}",))
                    for m in range(4):
                        zb = z2s[m % 2]
                        E.op("vector",
                             (lambda v, m=m, zb=zb, dbase=dbase:
                              v.tensor_copy(zb[:], dps[dbase + m][:])),
                             reads=(f"dps{dbase + m}",), writes=(f"z2s{m % 2}",))
                        E.op("sync",
                             (lambda s, tt=tt, m=m, n=n, zb=zb: s.dma_start(
                                 out=d_zz[tt * FT2 + m * 128:tt * FT2 + (m + 1) * 128,
                                          n * 512:(n + 1) * 512],
                                 in_=zb[:])),
                             reads=(f"z2s{m % 2}",), writes=(f"zz:{tt}:{m}:{n}",),
                             dma="zzw")
            emit_C(3)

        E.finalize()

        eng_sems = {}
        dma_sems = {}
        for e in ("tensor", "vector", "scalar"):
            eng_sems[e] = C(nc.semaphore(f"sem_{e}"))
        for sk in E.dma_counts:
            dma_sems[sk] = C(nc.semaphore(f"ds_{sk}"))

        block = C(nc.Block())
        n_out = E.dma_counts["outw"]

        @block.tensor
        def _(t):
            E.replay("tensor", t, eng_sems, dma_sems)

        @block.vector
        def _(v):
            E.replay("vector", v, eng_sems, dma_sems)

        @block.scalar
        def _(s):
            E.replay("scalar", s, eng_sems, dma_sems)

        @block.gpsimd
        def _(g):
            E.replay("gpsimd", g, eng_sems, dma_sems)

        @block.sync
        def _(s):
            E.replay("sync", s, eng_sems, dma_sems)
            s.wait_ge(dma_sems["outw"], 16 * n_out)

    return nc


def _ln_pair(E, z1b, zkeys, y1b, lns, praw, d_y1, p):
    S1, SSQ, MEAN, MSQ, M2, VAR, VPE, SD, R0, RR, H, H2, RSTD, MR, NMR = range(15)

    def col(c, i):
        return lns[:, c + 16 * i:c + 16 * i + 1]

    def vop(i, fn, rd, wr):
        E.op("vector", fn, reads=rd, writes=wr)

    for i in range(2):
        z, zk = z1b[i], zkeys[i]
        E.op("vector", (lambda v, z=z, i=i: v.reduce_sum(col(S1, i), z[:], axis=AX.X)),
             reads=(zk,), writes=(f"ln{i}:s1",))
    for i in range(2):
        z, zk = z1b[i], zkeys[i]
        E.op("scalar",
             (lambda s, z=z, i=i: s.activation(praw[:, 2048:2048 + D], z[:], AF.Square,
                                               accum_out=col(SSQ, i))),
             reads=(zk,), writes=("praw", f"ln{i}:ssq"))
    for i in range(2):
        E.op("vector", (lambda v, i=i: v.tensor_scalar_mul(col(MEAN, i), col(S1, i), 1.0 / D)),
             reads=(f"ln{i}:s1",), writes=(f"ln{i}:mean",))
        E.op("vector", (lambda v, i=i: v.tensor_scalar_mul(col(MSQ, i), col(SSQ, i), 1.0 / D)),
             reads=(f"ln{i}:ssq",), writes=(f"ln{i}:msq",))
        E.op("vector", (lambda v, i=i: v.tensor_mul(col(M2, i), col(MEAN, i), col(MEAN, i))),
             reads=(f"ln{i}:mean",), writes=(f"ln{i}:m2",))
        E.op("vector", (lambda v, i=i: v.tensor_sub(col(VAR, i), col(MSQ, i), col(M2, i))),
             reads=(f"ln{i}:msq", f"ln{i}:m2"), writes=(f"ln{i}:var",))
        E.op("vector", (lambda v, i=i: v.tensor_scalar_add(col(VPE, i), col(VAR, i), EPS)),
             reads=(f"ln{i}:var",), writes=(f"ln{i}:vpe",))
    for i in range(2):
        E.op("scalar", (lambda s, i=i: s.activation(col(SD, i), col(VPE, i), AF.Sqrt)),
             reads=(f"ln{i}:vpe",), writes=(f"ln{i}:sd",))
    for i in range(2):
        E.op("vector", (lambda v, i=i: v.reciprocal(col(R0, i), col(SD, i))),
             reads=(f"ln{i}:sd",), writes=(f"ln{i}:r0",))
        E.op("vector", (lambda v, i=i: v.tensor_mul(col(RR, i), col(R0, i), col(R0, i))),
             reads=(f"ln{i}:r0",), writes=(f"ln{i}:rr",))
        E.op("vector", (lambda v, i=i: v.tensor_mul(col(H, i), col(RR, i), col(VPE, i))),
             reads=(f"ln{i}:rr", f"ln{i}:vpe"), writes=(f"ln{i}:h",))
        E.op("vector", (lambda v, i=i: v.tensor_scalar(col(H2, i), col(H, i), -0.5, 1.5,
                                                       ALU.mult, ALU.add)),
             reads=(f"ln{i}:h",), writes=(f"ln{i}:h2",))
        E.op("vector", (lambda v, i=i: v.tensor_mul(col(RSTD, i), col(R0, i), col(H2, i))),
             reads=(f"ln{i}:r0", f"ln{i}:h2"), writes=(f"ln{i}:rstd",))
        E.op("vector", (lambda v, i=i: v.tensor_mul(col(MR, i), col(MEAN, i), col(RSTD, i))),
             reads=(f"ln{i}:mean", f"ln{i}:rstd"), writes=(f"ln{i}:mr",))
        E.op("vector", (lambda v, i=i: v.tensor_scalar_mul(col(NMR, i), col(MR, i), -1.0)),
             reads=(f"ln{i}:mr",), writes=(f"ln{i}:nmr",))
    for i in range(2):
        j = 2 * p + i
        z, zk = z1b[i], zkeys[i]
        E.op("scalar",
             (lambda s, z=z, i=i: s.activation(y1b[:], z[:], AF.Identity,
                                               bias=col(NMR, i), scale=col(RSTD, i))),
             reads=(zk, f"ln{i}:nmr", f"ln{i}:rstd"), writes=("y1b",))
        E.op("sync",
             (lambda s, j=j: s.dma_start(
                 out=d_y1[j * 128:(j + 1) * 128, :], in_=y1b[:])),
             reads=("y1b",), writes=(f"y1d:{j}",), dma="y1w")


def _ln_norm(E, nc, z, zkey, yout, ykey, lns, sq_scratch, sqkey):
    S1, SSQ, MEAN, MSQ, M2, VAR, VPE, SD, R0, RR, H, H2, RSTD, MR, NMR = range(15)

    def col(c):
        return lns[:, c:c + 1]

    E.op("vector", lambda v: v.reduce_sum(col(S1), z[:], axis=AX.X),
         reads=(zkey,), writes=("ln:s1",))
    E.op("scalar",
         (lambda s: s.activation(sq_scratch[:, 0:D], z[:], AF.Square,
                                 accum_out=col(SSQ))),
         reads=(zkey,), writes=(sqkey, "ln:ssq"))
    E.op("vector", lambda v: v.tensor_scalar_mul(col(MEAN), col(S1), 1.0 / D),
         reads=("ln:s1",), writes=("ln:mean",))
    E.op("vector", lambda v: v.tensor_scalar_mul(col(MSQ), col(SSQ), 1.0 / D),
         reads=("ln:ssq",), writes=("ln:msq",))
    E.op("vector", lambda v: v.tensor_mul(col(M2), col(MEAN), col(MEAN)),
         reads=("ln:mean",), writes=("ln:m2",))
    E.op("vector", lambda v: v.tensor_sub(col(VAR), col(MSQ), col(M2)),
         reads=("ln:msq", "ln:m2"), writes=("ln:var",))
    E.op("vector", lambda v: v.tensor_scalar_add(col(VPE), col(VAR), EPS),
         reads=("ln:var",), writes=("ln:vpe",))
    E.op("scalar", lambda s: s.activation(col(SD), col(VPE), AF.Sqrt),
         reads=("ln:vpe",), writes=("ln:sd",))
    E.op("vector", lambda v: v.reciprocal(col(R0), col(SD)),
         reads=("ln:sd",), writes=("ln:r0",))
    E.op("vector", lambda v: v.tensor_mul(col(RR), col(R0), col(R0)),
         reads=("ln:r0",), writes=("ln:rr",))
    E.op("vector", lambda v: v.tensor_mul(col(H), col(RR), col(VPE)),
         reads=("ln:rr", "ln:vpe"), writes=("ln:h",))
    E.op("vector", lambda v: v.tensor_scalar(col(H2), col(H), -0.5, 1.5,
                                             ALU.mult, ALU.add),
         reads=("ln:h",), writes=("ln:h2",))
    E.op("vector", lambda v: v.tensor_mul(col(RSTD), col(R0), col(H2)),
         reads=("ln:r0", "ln:h2"), writes=("ln:rstd",))
    E.op("vector", lambda v: v.tensor_mul(col(MR), col(MEAN), col(RSTD)),
         reads=("ln:mean", "ln:rstd"), writes=("ln:mr",))
    E.op("vector", lambda v: v.tensor_scalar_mul(col(NMR), col(MR), -1.0),
         reads=("ln:mr",), writes=("ln:nmr",))
    E.op("scalar",
         (lambda s: s.activation(yout[:], z[:], AF.Identity,
                                 bias=col(NMR), scale=col(RSTD))),
         reads=(zkey, "ln:nmr", "ln:rstd"), writes=(ykey,))


_NC_CACHE = None


def _get_nc():
    global _NC_CACHE
    if _NC_CACHE is None:
        _NC_CACHE = build_nc()
    return _NC_CACHE


def kernel(L0_attn_x, L0_attn_Ws, L0_attn_Wa, L0_attn_Wo, L0_attn_bo,
           L0_ffn_W1, L0_ffn_W2, L0_ffn_W3):
    x = np.asarray(L0_attn_x, dtype=np.float32)
    Ws = np.asarray(L0_attn_Ws, dtype=np.float32)
    Wa = np.asarray(L0_attn_Wa, dtype=np.float32)
    Wo = np.asarray(L0_attn_Wo, dtype=np.float32)
    bo = np.asarray(L0_attn_bo, dtype=np.float32).reshape(1, D)
    W1 = np.asarray(L0_ffn_W1, dtype=np.float32)
    W2 = np.asarray(L0_ffn_W2, dtype=np.float32)
    W3 = np.asarray(L0_ffn_W3, dtype=np.float32)

    WoT = np.ascontiguousarray(Wo.T)
    def tile_up(W_):
        t = W_.T.reshape(16, 128, 64, 128)
        return np.ascontiguousarray(t.transpose(2, 1, 0, 3)).reshape(64, 128, 16 * 128)
    w1tp = tile_up(W1)
    w3tp = tile_up(W3)
    w2ch = np.ascontiguousarray(
        W2.reshape(8, 256, 64, 128).transpose(0, 3, 2, 1))

    H128 = _hadamard(128)
    I2H64 = np.kron(np.eye(2, dtype=np.float32), _hadamard(64)).astype(np.float32)
    ident = np.eye(128, dtype=np.float32)
    qi, ki = np.arange(128)[:, None], np.arange(128)[None, :]
    tri = np.where(ki <= qi, 0.0, NEG).astype(np.float32)
    mask_p = [np.concatenate([np.zeros((128, 128), np.float32), tri], axis=1),
              np.concatenate([tri, np.full((128, 128), NEG, np.float32)], axis=1)]

    nc = _get_nc()
    in_maps = []
    perms = []
    for c in range(8):
        b, P = c // 2, c % 2
        tiles = [31 - 2 * j - P for j in range(NSLOT)]
        perm = np.concatenate([np.arange(t * 128, (t + 1) * 128) for t in tiles])
        perms.append((b, perm))
        xb = x[b]
        xbT = np.ascontiguousarray(xb.T)
        in_maps.append({
            "xqT": np.ascontiguousarray(xbT[:, perm]),
            "xq": np.ascontiguousarray(xb[perm]),
            "xkT": xbT,
            "xk": np.ascontiguousarray(xb),
            "ws": Ws, "wa": Wa, "wot": WoT, "bo": bo,
            "w1tp": w1tp, "w3tp": w3tp, "w2ch": w2ch,
            "h128": H128, "i2h64": I2H64, "ident": ident,
            "mask": mask_p[P],
        })

    res = run_bass_kernel_spmd(nc, in_maps, core_ids=list(range(8)))
    out = np.empty((B, L, D), dtype=np.float32)
    for c in range(8):
        b, perm = perms[c]
        out[b][perm] = res.results[c]["out"]
    return out
